# version 08
# baseline (speedup 1.0000x reference)
import contextlib
import numpy as np
import ml_dtypes
import concourse.bass as bass
import concourse.mybir as mybir
from concourse.bass_utils import run_bass_kernel_spmd

F32 = mybir.dt.float32
BF16 = mybir.dt.bfloat16
I32 = mybir.dt.int32
I16 = mybir.dt.int16
AF = mybir.ActivationFunctionType
ALU = mybir.AluOpType
AXX = mybir.AxisListType.X

S = 2048
D = 1024
H = 16
DH = 64
NCH = 8
INNER = 4096
NT = 16
NJ = 4
CSTAB = 25.0
NW = INNER * D
NSUP = 4
NTS = NT // NSUP

CONST_VALS = [0.0, 1.0, -1.0, 0.5, -0.5, 0.125, -CSTAB, 512.0, -512.0,
              1.0 / D, 1e-5, 127.0, 1.0 / 127.0, 1.0 / NW, 1048576.0, 2.0 / 127.0]
CIDX = {v: i for i, v in enumerate(CONST_VALS)}

DEBUG_TENSORS = {
    "qrope": ([128, NCH * S], "bf"),
    "krope": ([128, NCH * S], "bf"),
    "vsb": ([128, NT * H * 65], "bf"),
    "aor": ([128, NCH * S], "bf"),
    "denall": ([16, S], "f32"),
    "Etap": ([128, 16 * 512], "bf"),
    "mtap": ([128, 4 * 512], "bf"),
    "itap": ([128, 128], "bf"),
    "avtap": ([65, 512], "f32"),
    "rtap": ([2, S], "f32"),
    "recip": ([16, S], "f32"),
    "denall2": ([16, S], "f32"),
    "denvec": ([16, S], "f32"),
    "mcp": ([128, 512], "f32"),
    "sctap": ([128, 512], "f32"),
    "dent": ([128, S], "f32"),
    "aon": ([128, NCH * S], "bf"),
    "y1n": ([S, D], "f32"),
    "sttap": ([128, 16], "f32"),
    "xq": ([D, S], "bf"),
    "terng": ([D, INNER], "bf"),
    "ternd": ([INNER, D], "bf"),
    "hq": ([128, NSUP * NTS * INNER], "bf"),
}


def build_program(stop_phase=99, debug=(), max_pairs=None, dbg_mode=None):
    nc = bass.Bass("TRN2", target_bir_lowering=False)

    d_xT = nc.dram_tensor("xT", [D, S], F32, kind="ExternalInput").ap()
    d_x = nc.dram_tensor("x", [S, D], F32, kind="ExternalInput").ap()
    d_wqkv = nc.dram_tensor("wqkv", [D, 3 * D], F32, kind="ExternalInput").ap()
    d_wout = nc.dram_tensor("wout", [D, D], F32, kind="ExternalInput").ap()
    d_wgT = nc.dram_tensor("wgT", [D, INNER], F32, kind="ExternalInput").ap()
    d_wuT = nc.dram_tensor("wuT", [D, INNER], F32, kind="ExternalInput").ap()
    d_wdT = nc.dram_tensor("wdT", [INNER, D], F32, kind="ExternalInput").ap()
    d_consts = nc.dram_tensor("consts", [128, len(CONST_VALS)], F32, kind="ExternalInput").ap()
    d_cos = nc.dram_tensor("cosf", [128, S], F32, kind="ExternalInput").ap()
    d_sin = nc.dram_tensor("sinf", [128, S], F32, kind="ExternalInput").ap()
    d_ident = nc.dram_tensor("ident", [128, 128], BF16, kind="ExternalInput").ap()
    d_masks = nc.dram_tensor("masks", [128, 4 * 512], BF16, kind="ExternalInput").ap()
    d_sel2 = nc.dram_tensor("sel2", [2, 128], F32, kind="ExternalInput").ap()
    d_ones1 = nc.dram_tensor("ones1", [1, 128], F32, kind="ExternalInput").ap()
    d_out = nc.dram_tensor("out", [S, D], F32, kind="ExternalOutput").ap()
    d_terng = nc.dram_tensor("terng", [D, INNER], BF16, kind="Internal").ap()
    d_ternu = nc.dram_tensor("ternu", [D, INNER], BF16, kind="Internal").ap()
    d_ternd = nc.dram_tensor("ternd", [INNER, D], BF16, kind="Internal").ap()
    d_y1s = nc.dram_tensor("y1s", [S, D], F32, kind="Internal").ap()

    d_xqs = nc.dram_tensor("xqs", [D, S], BF16, kind="Internal").ap()

    dbg = {}
    for name in debug:
        shp, dt = DEBUG_TENSORS[name]
        dbg[name] = nc.dram_tensor("dbg_" + name, shp,
                                   BF16 if dt == "bf" else F32,
                                   kind="ExternalOutput").ap()

    es = contextlib.ExitStack()
    with es:
        s_consts = es.enter_context(nc.sbuf_tensor("c_consts", [128, len(CONST_VALS)], F32, side="right"))
        s_ident = es.enter_context(nc.sbuf_tensor("c_ident", [128, 128], BF16, side="right"))
        s_amx = es.enter_context(nc.sbuf_tensor("p_amx", [128, NT], F32, side="right"))
        s_thr = es.enter_context(nc.sbuf_tensor("p_thr", [128, 8], F32, side="right"))

        def CC(v):
            return s_consts[:, CIDX[v]:CIDX[v] + 1]

        for i, v in enumerate(CONST_VALS):
            nc.const_aps.register(F32, v, s_consts[:, i:i + 1])

        class Ctr:
            def __init__(self, name):
                self.h = es.enter_context(nc.semaphore(name))
                self.n = 0

            def inc(self, instr, k=1):
                instr.then_inc(self.h, k)
                self.n += k
                return self.n

            def dinc(self, instr):
                return self.inc(instr, 16)

        dS = Ctr("dS")
        dC2 = Ctr("dC2")
        dG = Ctr("dG")
        dO = Ctr("dO")
        dAor = Ctr("dAor")
        dDen = [Ctr("dDen0"), Ctr("dDen1")]
        dRst = [Ctr("dRst0"), Ctr("dRst1")]
        dWo = Ctr("dWo")
        dX = [Ctr("dX0"), Ctr("dX1")]
        dY1n = [Ctr("dY1n0"), Ctr("dY1n1")]
        dXq = [Ctr("dXq0"), Ctr("dXq1")]
        dW5 = [Ctr("dW50"), Ctr("dW51"), Ctr("dW52")]
        tpA = Ctr("tpA")
        thrV = Ctr("thrV")
        thrT = Ctr("thrT")
        ts2 = Ctr("ts2")
        tv2 = Ctr("tv2")
        dTst = [Ctr(f"dTst{i}") for i in range(4)]
        dAcc = [Ctr("dAcc0"), Ctr("dAcc1")]
        dXql = [Ctr("dXql0"), Ctr("dXql1")]
        dGu = [Ctr("dGu0"), Ctr("dGu1")]
        dTd = [Ctr("dTd0"), Ctr("dTd1")]
        dY1l = [Ctr("dY1l0"), Ctr("dY1l1")]
        dOut = Ctr("dOut")
        te = Ctr("te")
        ve = Ctr("ve")
        se = Ctr("se")
        ge = Ctr("ge")

        block = es.enter_context(nc.Block())

        def _scope():
            return es.enter_context(contextlib.ExitStack())

        scB = _scope()
        scA = _scope()
        psA = _scope()

        def mk(stack, name, shape, dt=F32, side=None):
            kw = {} if side is None else {"side": side}
            return stack.enter_context(nc.sbuf_tensor(name, shape, dt, **kw))

        s_q = mk(scB, "a_q", [128, NCH * S], BF16)
        s_k = mk(scB, "a_k", [128, NCH * S], BF16)
        s_v = mk(scB, "a_v", [128, NT * H * 65], BF16)
        s_cos = mk(scA, "a_cos", [128, S])
        s_sin = mk(scA, "a_sin", [128, S])
        s_xfm = mk(scA, "a_xfm", [128, NCH * S], BF16)
        s_wqk = mk(scA, "a_wqk", [128, NCH * 2048], BF16)
        s_wv = mk(scA, "a_wv", [128, NCH * 1024], BF16)
        s_tmpA = mk(scA, "a_tmpA", [128, S], BF16)
        s_tmpB = mk(scA, "a_tmpB", [128, S], BF16)
        s_swp = mk(scA, "a_swp", [128, S], BF16)

        p_qk = [psA.enter_context(nc.psum_tensor(f"p_qk{i}", [128, 512], F32))
                for i in range(4)]

        @block.sync
        def _(sy):
            for dst, src in [(s_consts[:], d_consts), (s_ident[:], d_ident),
                             (s_cos[:], d_cos), (s_sin[:], d_sin)]:
                dS.dinc(sy.dma_start(out=dst, in_=src))
        ph0_sync = dS.n

        @block.gpsimd
        def _(g):
            for c in range(NCH):
                dG.dinc(g.dma_start(out=s_xfm[:, c * S:(c + 1) * S],
                                    in_=d_xT[c * 128:(c + 1) * 128, :]))
            for c in range(NCH):
                dG.dinc(g.dma_start(out=s_wqk[:, c * 2048:(c + 1) * 2048],
                                    in_=d_wqkv[c * 128:(c + 1) * 128, 0:2048]))
                dG.dinc(g.dma_start(out=s_wv[:, c * 1024:(c + 1) * 1024],
                                    in_=d_wqkv[c * 128:(c + 1) * 128, 2048:3072]))
        ph0_gp = dG.n

        ve0, se0, te0 = ve.n, se.n, te.n

        @block.tensor
        def _(t):
            t.wait_ge(dS.h, ph0_sync)
            t.wait_ge(dG.h, ph0_gp)
            gi = 0
            for sl in range(16):
                for ntile in range(4):
                    pb = p_qk[gi % 4]
                    if gi >= 4:
                        t.wait_ge(se.h, se0 + gi - 3)
                    for kc in range(NCH):
                        mm = t.matmul(pb[:],
                                      s_wqk[:, kc * 2048 + sl * 128: kc * 2048 + (sl + 1) * 128],
                                      s_xfm[:, kc * S + ntile * 512: kc * S + (ntile + 1) * 512],
                                      start=(kc == 0), stop=(kc == NCH - 1))
                    te.inc(mm)
                    gi += 1
            for tt in range(NT):
                for vs in range(2):
                    pb = p_qk[gi % 4]
                    if gi - 4 < 64:
                        t.wait_ge(se.h, se0 + gi - 3)
                    else:
                        t.wait_ge(ve.h, ve0 + NT + (gi - 4 - 64) + 1)
                    for kc in range(NCH):
                        mm = t.matmul(pb[:],
                                      s_xfm[:, kc * S + tt * 128: kc * S + (tt + 1) * 128],
                                      s_wv[:, kc * 1024 + vs * 512: kc * 1024 + (vs + 1) * 512],
                                      start=(kc == 0), stop=(kc == NCH - 1))
                    te.inc(mm)
                    gi += 1
        te_p1 = te.n

        @block.scalar
        def _(s):
            for gi in range(64):
                s.wait_ge(te.h, te0 + gi + 1)
                sl, ntile = divmod(gi, 4)
                dst = s_q if sl < 8 else s_k
                c = sl % 8
                se.inc(s.activation(dst[:, c * S + ntile * 512: c * S + (ntile + 1) * 512],
                                    p_qk[gi % 4][:], AF.Copy))
        se_p1 = se.n

        @block.vector
        def _(v):
            for t_ in range(NT):
                ap = s_v[:, t_ * H * 65:(t_ + 1) * H * 65].rearrange(
                    "p (h w) -> p h w", h=H)[:, :, 64:65]
                v.memset(ap, 1.0)
            ve.inc(v.nop(), NT)
            for gi in range(64, 96):
                v.wait_ge(te.h, te0 + gi + 1)
                tt, vs = divmod(gi - 64, 2)
                base = tt * H * 65
                pb = p_qk[gi % 4]
                dst = s_v[:, base + (vs * 8) * 65: base + (vs * 8 + 8) * 65].rearrange(
                    "p (h w) -> p h w", h=8)[:, :, 0:64]
                src = pb[:].rearrange("p (h w) -> p h w", h=8)
                ve.inc(v.tensor_copy(dst, src))
        ve_vevac = ve.n

        @block.gpsimd
        def _(g):
            for ci in range(16):
                qk, c = divmod(ci, 8)
                src = s_q if qk == 0 else s_k
                g.wait_ge(se.h, se0 + (qk * 8 + c) * 4 + 4)
                if ci >= 1:
                    g.wait_ge(ve.h, ve_vevac + (ci - 1) * 2 + 1)
                for hh in range(2):
                    o = hh * 64
                    dO.dinc(g.dma_start(out=s_swp[o + 32: o + 64, :],
                                        in_=src[o: o + 32, c * S:(c + 1) * S]))
                    dO.dinc(g.dma_start(out=s_swp[o: o + 32, :],
                                        in_=src[o + 32: o + 64, c * S:(c + 1) * S]))
        dO_p1 = dO.n

        @block.vector
        def _(v):
            for ci in range(16):
                qk, c = divmod(ci, 8)
                src = s_q if qk == 0 else s_k
                sl = src[:, c * S:(c + 1) * S]
                v.wait_ge(dO.h, (ci + 1) * 64)
                v.drain()
                v.tensor_tensor(s_tmpA[:], sl, s_cos[:], ALU.mult)
                ve.inc(v.tensor_tensor(s_tmpB[:], s_swp[:], s_sin[:], ALU.mult))
                v.drain()
                ve.inc(v.tensor_tensor(sl, s_tmpA[:], s_tmpB[:], ALU.add))
        ve_rope = ve.n

        if "qrope" in dbg:
            @block.sync
            def _(sy):
                sy.wait_ge(ve.h, ve_rope)
                dS.dinc(sy.dma_start(out=dbg["qrope"], in_=s_q[:]))
                dS.dinc(sy.dma_start(out=dbg["krope"], in_=s_k[:]))
        if "vsb" in dbg:
            @block.sync
            def _(sy):
                sy.wait_ge(ve.h, ve_vevac)
                dS.dinc(sy.dma_start(out=dbg["vsb"], in_=s_v[:]))

        ctrs = ([dS, dC2, dG, dO, dAor, dWo, dTst[0], dTst[1], dTst[2], dTst[3], dAcc[0], dAcc[1], dXql[0], dXql[1], dOut, te, ve, se, ge, tpA, thrV, thrT, ts2, tv2]
                + dDen + dRst + dX + dY1n + dXq + dW5 + dGu + dTd + dY1l)
        if stop_phase <= 1:
            _finish(block, ctrs)
            return nc

        scA.close()
        psA.close()
        scD = _scope()
        scC = _scope()
        psB = _scope()
        s_E = mk(scC, "a_E", [128, 16 * 512], BF16)
        s_masks = mk(scC, "a_masks", [128, 4 * 512], BF16)

        @block.sync
        def _(sy):
            sy.wait_ge(te.h, te_p1)
            sy.wait_ge(ve.h, ve_rope)
            sy.wait_ge(dO.h, dO_p1)
            dC2.dinc(sy.dma_start(out=s_masks[:], in_=d_masks))
        dS_masks = dC2.n
        s_dent = [mk(scC, f"a_dent{i}", [128, S]) for i in range(2)]
        s_odstb = mk(scC, "a_odstb", [128, 512], BF16)
        s_aor = mk(scD, "d_aor", [128, NCH * S], BF16, side="right")
        s_denall = mk(scD, "d_denall", [16, S], side="right")
        s_wst = [mk(scD, f"t_wst{i}", [128, 1024], side="right")
                 for i in range(3)]
        s_msum = mk(scD, "t_msum", [128, 96], side="right")
        s_red = mk(scD, "t_red", [128, 3], side="right")
        s_t11 = mk(scD, "t_t11", [1, 1], side="right")
        s_ones1 = mk(scD, "t_ones1", [1, 128], side="right")
        s_ta = [mk(scD, f"t_ta{i}", [128, 1024], BF16, side="right")
                for i in range(2)]
        s_tb = [mk(scD, f"t_tb{i}", [128, 1024], BF16, side="right")
                for i in range(2)]
        WCHUNKS = []
        for mi6, (dsrc6, ddst6, ntl6, wid6) in enumerate(
                ((d_wgT, d_terng, 8, 4096), (d_wuT, d_ternu, 8, 4096),
                 (d_wdT, d_ternd, 32, 1024))):
            for ti6 in range(ntl6):
                for cc6 in range(wid6 // 1024):
                    WCHUNKS.append((mi6,
                                    dsrc6[ti6 * 128:(ti6 + 1) * 128,
                                          cc6 * 1024:(cc6 + 1) * 1024],
                                    ddst6[ti6 * 128:(ti6 + 1) * 128,
                                          cc6 * 1024:(cc6 + 1) * 1024]))
        NWCH = len(WCHUNKS)

        def pvW(L):
            return ((L // 3) + 1) * 16

        @block.sync
        def _(sy):
            dS.dinc(sy.dma_start(out=s_ones1[:], in_=d_ones1))
        dS_ones1 = dS.n
        p_sc = [psB.enter_context(nc.psum_tensor(f"p_sc{i}", [128, 512], F32))
                for i in range(4)]
        p_av = [psB.enter_context(nc.psum_tensor(f"p_av{i}", [65, 512], F32))
                for i in range(2)]

        pairs = [(h, j) for h in range(H) for j in range(NJ)]
        if max_pairs is not None:
            pairs = pairs[:max_pairs]
        nkc = {p: 4 * p[1] + 4 for p in pairs}
        se_base = {}
        run = se.n
        for p in pairs:
            se_base[p] = run
            run += nkc[p]
        se_p2_end = run
        te_sc_base = {}
        te_av_done = {}
        trun = te.n
        for p in pairs:
            te_sc_base[p] = trun
            trun += nkc[p] + 1
            te_av_done[p] = trun
        ve_evac = {}
        vrun = ve.n
        for i, p in enumerate(pairs):
            ve_evac[p] = vrun + i + 1
        ve_p2_end = vrun + len(pairs)
        dO_evt = {}
        arun = dAor.n
        drun = [dDen[0].n, dDen[1].n]
        for h in range(H):
            for j in range(NJ):
                if (h, j) not in nkc:
                    continue
                if h % 2 == 1:
                    arun += 16
                    dO_evt[("aor", h, j)] = arun
            if (h, NJ - 1) in nkc:
                drun[h % 2] += 16
                dO_evt[("den", h)] = (h % 2, drun[h % 2])


        @block.tensor
        def _(t):
            t.wait_ge(ve.h, ve_rope)
            t.wait_ge(dC2.h, 32)
            for pi, (h, j) in enumerate(pairs):
                ch, hp = divmod(h, 2)
                for kc in range(nkc[(h, j)]):
                    pb = p_sc[kc % 4]
                    if kc >= 4:
                        t.wait_ge(se.h, se_base[(h, j)] + kc - 3)
                    elif pi > 0:
                        prev = pairs[pi - 1]
                        kcp = max(kk for kk in range(nkc[prev]) if kk % 4 == kc % 4)
                        t.wait_ge(se.h, se_base[prev] + kcp + 1)
                    r = kc - 4 * j
                    if dbg_mode == "nomask":
                        mm = t.matmul(pb[:],
                                      s_k[hp * 64: hp * 64 + 64,
                                          ch * S + kc * 128: ch * S + (kc + 1) * 128],
                                      s_q[hp * 64: hp * 64 + 64,
                                          ch * S + j * 512: ch * S + (j + 1) * 512],
                                      start=True, stop=True)
                    elif dbg_mode == "noscores":
                        mm = t.matmul(pb[:], s_ident[:],
                                      s_masks[:, max(r, 0) * 512:(max(r, 0) + 1) * 512],
                                      start=True, stop=True)
                    else:
                        mm = t.matmul(pb[:],
                                      s_k[hp * 64: hp * 64 + 64,
                                          ch * S + kc * 128: ch * S + (kc + 1) * 128],
                                      s_q[hp * 64: hp * 64 + 64,
                                          ch * S + j * 512: ch * S + (j + 1) * 512],
                                      start=True, stop=(r < 0))
                        if r >= 0:
                            mm = t.matmul(pb[:], s_ident[:],
                                          s_masks[:, r * 512:(r + 1) * 512],
                                          start=False, stop=True)
                    te.inc(mm)
                pv = p_av[pi % 2]
                if pi >= 2:
                    t.wait_ge(ve.h, ve_evac[pairs[pi - 2]])
                for kc in range(nkc[(h, j)]):
                    t.wait_ge(se.h, se_base[(h, j)] + kc + 1)
                    vbase = kc * H * 65 + h * 65
                    mm = t.matmul(pv[:], s_v[:, vbase: vbase + 65],
                                  s_E[:, kc * 512:(kc + 1) * 512],
                                  start=(kc == 0), stop=(kc == nkc[(h, j)] - 1))
                te.inc(mm)
        assert te.n == trun, (te.n, trun)

        @block.scalar
        def _(s):
            for (h, j) in pairs:
                for kc in range(nkc[(h, j)]):
                    s.wait_ge(te.h, te_sc_base[(h, j)] + kc + 1)
                    se.inc(s.activation(s_E[:, kc * 512:(kc + 1) * 512],
                                        p_sc[kc % 4][:], AF.Exp,
                                        bias=-CSTAB, scale=0.125))
        assert se.n == se_p2_end

        tern_emitA = [0]

        @block.vector
        def _(v):
            def _passA(v, hi):
                while tern_emitA[0] < hi:
                    i6 = tern_emitA[0]
                    v.wait_ge(dW5[i6 % 3].h, pvW(i6))
                    tpA.inc(v.tensor_reduce(s_msum[:, i6:i6 + 1],
                                            s_wst[i6 % 3][:], AXX, ALU.add,
                                            apply_absolute_value=True))
                    tern_emitA[0] += 1
            for pi, (h, j) in enumerate(pairs):
                ch, hp = divmod(h, 2)
                pv = p_av[pi % 2]
                v.wait_ge(te.h, te_av_done[(h, j)])
                if h % 2 == 0:
                    v.tensor_copy(s_aor[0:64, ch * S + j * 512: ch * S + (j + 1) * 512],
                                  pv[0:64, :])
                else:
                    if j > 0:
                        v.wait_ge(dAor.h, dO_evt[("aor", h, j - 1)])
                    elif h >= 3:
                        v.wait_ge(dAor.h, dO_evt[("aor", h - 2, NJ - 1)])
                    v.tensor_copy(s_odstb[0:64, :], pv[0:64, :])
                if h >= 2:
                    par, val = dO_evt[("den", h - 2)]
                    v.wait_ge(dDen[par].h, val)
                ve.inc(v.tensor_copy(s_dent[h % 2][64:65, j * 512:(j + 1) * 512],
                                     pv[64:65, :]))
                _passA(v, ((pi + 1) * NWCH) // len(pairs))
            _passA(v, NWCH)
        assert ve.n == ve_p2_end
        assert tpA.n == NWCH

        tern_emitL = [0]

        @block.gpsimd
        def _(g):
            def _loadA(g, hi):
                while tern_emitL[0] < hi:
                    L = tern_emitL[0]
                    if L >= 3:
                        g.wait_ge(tpA.h, L - 2)
                    dW5[L % 3].dinc(g.dma_start(out=s_wst[L % 3][:],
                                                in_=WCHUNKS[L][1]))
                    tern_emitL[0] += 1
            slot = 0
            for h in range(H):
                ch = h // 2
                for j in range(NJ):
                    if (h, j) not in ve_evac:
                        continue
                    if h % 2 == 1:
                        g.wait_ge(ve.h, ve_evac[(h, j)])
                        dAor.dinc(g.dma_start(
                            out=s_aor[64:128, ch * S + j * 512: ch * S + (j + 1) * 512],
                            in_=s_odstb[0:64, :]))
                    slot += 1
                    _loadA(g, (slot * NWCH) // len(pairs))
                if (h, NJ - 1) in ve_evac:
                    g.wait_ge(ve.h, ve_evac[(h, NJ - 1)])
                    dDen[h % 2].dinc(g.dma_start(out=s_denall[h:h + 1, :],
                                                 in_=s_dent[h % 2][64:65, :]))
            _loadA(g, NWCH)
        dO_p2_end = (dAor.n, dDen[0].n, dDen[1].n)

        dS_aortap = None
        if "aor" in dbg:
            @block.sync
            def _(sy):
                sy.wait_ge(ve.h, ve_p2_end)
                sy.wait_ge(dAor.h, dO_p2_end[0])
                dS.dinc(sy.dma_start(out=dbg["aor"], in_=s_aor[:]))
            dS_aortap = dS.n
        if "denall" in dbg:
            @block.sync
            def _(sy):
                sy.wait_ge(dDen[0].h, dO_p2_end[1])
                sy.wait_ge(dDen[1].h, dO_p2_end[2])
                dS.dinc(sy.dma_start(out=dbg["denall"], in_=s_denall[:]))
        if "sctap" in dbg:
            s_sctap = mk(scC, "a_sctap", [128, 512])

            @block.vector
            def _(v):
                v.wait_ge(se.h, se_p2_end)
                ve.inc(v.tensor_copy(s_sctap[:], p_sc[3][:]))
            _svc = ve.n

            @block.sync
            def _(sy):
                sy.wait_ge(ve.h, _svc)
                dS.dinc(sy.dma_start(out=dbg["sctap"], in_=s_sctap[:]))
        if "mcp" in dbg:
            s_mcp = mk(scC, "a_mcp", [128, 512])
            p_dbgm = psB.enter_context(nc.psum_tensor("p_dbgm", [128, 512], F32))

            @block.tensor
            def _(t):
                t.wait_ge(dC2.h, dS_masks)
                te.inc(t.matmul(p_dbgm[:], s_ident[:], s_masks[:, 3 * 512:4 * 512],
                                start=True, stop=True))
            _tmc = te.n

            @block.vector
            def _(v):
                v.wait_ge(te.h, _tmc)
                ve.inc(v.tensor_copy(s_mcp[:], p_dbgm[:]))
            _vmc = ve.n

            @block.sync
            def _(sy):
                sy.wait_ge(ve.h, _vmc)
                dS.dinc(sy.dma_start(out=dbg["mcp"], in_=s_mcp[:]))
        if "avtap" in dbg:
            s_avt = mk(scC, "a_avt", [65, 512])

            @block.vector
            def _(v):
                v.wait_ge(te.h, te_av_done[pairs[0]])
                ve.inc(v.tensor_copy(s_avt[:], p_av[0][:]))
            _vav = ve.n

            @block.sync
            def _(sy):
                sy.wait_ge(ve.h, _vav)
                dS.dinc(sy.dma_start(out=dbg["avtap"], in_=s_avt[:]))
        if "itap" in dbg:
            @block.sync
            def _(sy):
                sy.wait_ge(dS.h, ph0_sync)
                dS.dinc(sy.dma_start(out=dbg["itap"], in_=s_ident[:]))
        if "mtap" in dbg:
            @block.sync
            def _(sy):
                sy.wait_ge(dC2.h, dS_masks)
                dS.dinc(sy.dma_start(out=dbg["mtap"], in_=s_masks[:]))
        if "Etap" in dbg:
            @block.sync
            def _(sy):
                sy.wait_ge(ve.h, ve_p2_end)
                dS.dinc(sy.dma_start(out=dbg["Etap"], in_=s_E[:]))
        if "dent" in dbg:
            @block.sync
            def _(sy):
                sy.wait_ge(ve.h, ve_p2_end)
                dS.dinc(sy.dma_start(out=dbg["dent"], in_=s_dent[0][:]))

        if stop_phase <= 2:
            _finish(block, ctrs)
            return nc

        scC.close()
        scE = _scope()
        s_recip = mk(scE, "e_recip", [16, S])
        s_R = [mk(scE, f"e_R{i}", [128, S]) for i in range(2)]

        s_dencp = mk(scE, "e_dencp", [16, S]) if "denvec" in dbg else None

        @block.vector
        def _(v):
            v.wait_ge(dAor.h, dO_p2_end[0])
            v.wait_ge(dDen[0].h, dO_p2_end[1])
            v.wait_ge(dDen[1].h, dO_p2_end[2])
            if s_dencp is not None:
                v.tensor_copy(s_dencp[:], s_denall[:])
            c20 = s_consts[0:16, CIDX[1048576.0]:CIDX[1048576.0] + 1]
            c0_16 = s_consts[0:16, CIDX[0.0]:CIDX[0.0] + 1]
            v.tensor_scalar(s_recip[:], s_denall[:], c20, c0_16, ALU.mult, ALU.add)
            v.drain()
            v.reciprocal(s_recip[:], s_recip[:])
            v.drain()
            ve.inc(v.tensor_scalar(s_recip[:], s_recip[:], c20, c0_16,
                                   ALU.mult, ALU.add))
        ve_recip = ve.n

        psB.close()
        s_sel2 = mk(scE, "e_sel2", [2, 128])
        s_rst = [mk(scE, f"e_rst{i}", [2, S]) for i in range(2)]
        p_R = [scE.enter_context(nc.psum_tensor(f"p_R{i}", [128, 512], F32))
               for i in range(2)]

        @block.sync
        def _(sy):
            dC2.dinc(sy.dma_start(out=s_sel2[:], in_=d_sel2))
        dS_sel2 = dC2.n

        te_R = {}
        trunR = te.n
        for c in range(NCH):
            for qq in range(4):
                trunR += 1
                te_R[(c, qq)] = trunR
        ve_normq = {}
        vrun = ve.n
        for c in range(NCH):
            for qq in range(4):
                vrun += 1
                ve_normq[(c, qq)] = vrun
        dO_rst = {}
        _rr = [dRst[0].n, dRst[1].n]
        for c in range(NCH):
            _rr[c % 2] += 16
            dO_rst[c] = (c % 2, _rr[c % 2])

        @block.gpsimd
        def _(g):
            for c in range(NCH):
                g.wait_ge(ve.h, ve_recip)
                if c >= 2:
                    g.wait_ge(te.h, te_R[(c - 2, 3)])
                dRst[c % 2].dinc(g.dma_start(out=s_rst[c % 2][:],
                                             in_=s_recip[2 * c:2 * c + 2, :]))

        @block.tensor
        def _(t):
            t.wait_ge(dC2.h, 32)
            for c in range(NCH):
                t.wait_ge(dRst[c % 2].h, dO_rst[c][1])
                for qq in range(4):
                    if (c, qq) != (0, 0):
                        pass
                    pb = p_R[(c * 4 + qq) % 2]
                    idx = c * 4 + qq - 2
                    if idx >= 0:
                        t.wait_ge(ve.h, ve_normq[(idx // 4, idx % 4)])
                    te.inc(t.matmul(pb[:], s_sel2[:],
                                    s_rst[c % 2][:, qq * 512:(qq + 1) * 512],
                                    start=True, stop=True))

        @block.vector
        def _(v):
            if dS_aortap is not None:
                v.wait_ge(dS.h, dS_aortap)
            for c in range(NCH):
                for qq in range(4):
                    v.wait_ge(te.h, te_R[(c, qq)])
                    pb = p_R[(c * 4 + qq) % 2]
                    sl = s_aor[:, c * S + qq * 512: c * S + (qq + 1) * 512]
                    ve.inc(v.tensor_tensor(sl, sl, pb[:], ALU.mult))
        ve_p3 = ve.n

        if "denall2" in dbg:
            @block.sync
            def _(sy):
                sy.wait_ge(ve.h, ve_recip)
                dS.dinc(sy.dma_start(out=dbg["denall2"], in_=s_denall[:]))
        if "denvec" in dbg:
            @block.sync
            def _(sy):
                sy.wait_ge(ve.h, ve_recip)
                dS.dinc(sy.dma_start(out=dbg["denvec"], in_=s_dencp[:]))
        if "rtap" in dbg:
            @block.sync
            def _(sy):
                sy.wait_ge(ve.h, ve_p3)
                dS.dinc(sy.dma_start(out=dbg["rtap"], in_=s_rst[0][:]))
                dS.dinc(sy.dma_start(out=dbg["recip"], in_=s_recip[:]))
        if "aon" in dbg:
            @block.sync
            def _(sy):
                sy.wait_ge(ve.h, ve_p3)
                dS.dinc(sy.dma_start(out=dbg["aon"], in_=s_aor[:]))

        if stop_phase <= 3:
            _finish(block, ctrs)
            return nc

        scE.close()
        scB.close()
        scF = _scope()
        psC = _scope()
        s_wout = mk(scF, "f_wout", [128, NCH * D], BF16)
        s_xt = [mk(scF, f"f_xt{i}", [128, D]) for i in range(2)]
        s_y1 = mk(scF, "f_y1", [128, D])
        s_y1n = [mk(scF, f"f_y1n{i}", [128, D]) for i in range(2)]
        s_z = mk(scF, "f_z", [128, D])
        s_tr2 = mk(scF, "f_tr2", [128, D], BF16)
        s_i32 = mk(scF, "f_i32", [128, D], I32)
        s_r1 = mk(scF, "f_r1", [128, D])
        s_u = mk(scF, "f_u", [128, D], BF16)
        s_xqtm = mk(scF, "f_xqtm", [128, D], BF16)
        s_xqst = [mk(scF, f"f_xqst{i}", [128, D], BF16) for i in range(2)]
        s_st = mk(scF, "f_st", [128, 16])

        p_o = [psC.enter_context(nc.psum_tensor(f"p_o{i}", [128, 512], F32))
               for i in range(4)]
        p_tr = [psC.enter_context(nc.psum_tensor(f"p_tr{i}", [128, 512], BF16))
                for i in range(2)]
        p_m1 = psC.enter_context(nc.psum_tensor("p_m1b", [1, 1], F32))
        p_m2 = psC.enter_context(nc.psum_tensor("p_m2b", [128, 1], F32))

        ST = {k: i for i, k in enumerate(
            ["sum", "sumsq", "mean", "msq", "var", "rec", "rstd", "negmr",
             "amax", "rec2", "s1"])}

        def stc(name):
            return s_st[:, ST[name]:ST[name] + 1]

        VE_PER = 22
        SE_PER = 3
        ve4_0 = ve.n
        se4_0 = se.n
        te4_0 = te.n
        te_o_end = te4_0 + 2 * NT

        @block.gpsimd
        def _(g):
            g.wait_ge(te.h, trunR)
            for c in range(NCH):
                dWo.dinc(g.dma_start(out=s_wout[:, c * D:(c + 1) * D],
                                    in_=d_wout[c * 128:(c + 1) * 128, :]))
            for i6 in range(NWCH):
                L = NWCH + i6
                if i6 < 3:
                    g.wait_ge(tpA.h, NWCH)
                else:
                    g.wait_ge(ts2.h, 2 * (i6 - 3) + 2)
                dW5[L % 3].dinc(g.dma_start(out=s_wst[L % 3][:],
                                            in_=WCHUNKS[i6][1]))
                if i6 >= 1:
                    p6 = i6 - 1
                    g.wait_ge(ts2.h, 2 * p6 + 2)
                    dAcc[p6 % 2].dinc(g.dma_start(
                        out=s_ta[p6 % 2][:], in_=s_tb[p6 % 2][:],
                        accum_op=ALU.add))
                    g.wait_ge(dAcc[p6 % 2].h, ((p6 // 2) + 1) * 16)
                    dTst[p6 % 4].dinc(g.dma_start(
                        out=WCHUNKS[p6][2], in_=s_ta[p6 % 2][:]))
            p6 = NWCH - 1
            g.wait_ge(ts2.h, 2 * p6 + 2)
            dAcc[p6 % 2].dinc(g.dma_start(
                out=s_ta[p6 % 2][:], in_=s_tb[p6 % 2][:], accum_op=ALU.add))
            g.wait_ge(dAcc[p6 % 2].h, ((p6 // 2) + 1) * 16)
            dTst[p6 % 4].dinc(g.dma_start(
                out=WCHUNKS[p6][2], in_=s_ta[p6 % 2][:]))
        dG_wout = dWo.n

        dS_xland = {}
        dS_y1n = {}
        dS_xq = {}

        @block.sync
        def _(sy):
            sy.wait_ge(te.h, trunR)
            sy.wait_ge(ve.h, ve_p3)
            for nt in range(2):
                dX[nt % 2].dinc(sy.dma_start(out=s_xt[nt % 2][:],
                                             in_=d_x[nt * 128:(nt + 1) * 128, :]))
                dS_xland[nt] = (nt % 2, dX[nt % 2].n)
            for nt in range(NT):
                sy.wait_ge(se.h, se4_0 + nt * SE_PER + 3)
                dY1n[nt % 2].dinc(sy.dma_start(out=d_y1s[nt * 128:(nt + 1) * 128, :],
                                               in_=s_y1n[nt % 2][:]))
                dS_y1n[nt] = (nt % 2, dY1n[nt % 2].n)
                sy.wait_ge(ve.h, ve4_0 + nt * VE_PER + 22)
                st = s_xqst[nt % 2]
                outap = bass.AP(d_xqs.tensor, nt * 128,
                                [[S, 128], [S * 128, NCH], [1, 128]])
                inap = st[:].rearrange("p (c w) -> p c w", c=NCH)
                dXq[nt % 2].dinc(sy.dma_start(out=outap, in_=inap))
                dS_xq[nt] = (nt % 2, dXq[nt % 2].n)
                if nt + 2 < NT:
                    sy.wait_ge(ve.h, ve4_0 + nt * VE_PER + 2)
                    dX[nt % 2].dinc(sy.dma_start(out=s_xt[nt % 2][:],
                                                 in_=d_x[(nt + 2) * 128:(nt + 3) * 128, :]))
                    dS_xland[nt + 2] = (nt % 2, dX[nt % 2].n)
        dS_p4 = (dY1n[0].n, dY1n[1].n, dXq[0].n, dXq[1].n)

        te_evt = {}
        trun4 = te.n
        sched = []
        for nt in range(NT):
            sched.append(("o", nt))
            if nt >= 2:
                sched.append(("t", nt - 2))
        sched.append(("t", NT - 2))
        sched.append(("t", NT - 1))
        for kind, nt in sched:
            if kind == "o":
                for half in range(2):
                    trun4 += 1
                    te_evt[("o", nt, half)] = trun4
            else:
                for b in range(2):
                    trun4 += 1
                    te_evt[("t", nt, b)] = trun4

        @block.tensor
        def _(t):
            t.wait_ge(dWo.h, dG_wout)
            t.wait_ge(ve.h, ve_p3)
            t.wait_ge(dS.h, dS_ones1)
            for mi6 in range(3):
                t.wait_ge(thrV.h, 3 if mi6 == 0 else 2 * mi6 + 2)
                thrT.inc(t.matmul(p_m1[:], s_red[:, mi6:mi6 + 1], CC(1.0),
                                  start=True, stop=True))
                t.wait_ge(thrV.h, 2 * mi6 + 4)
                thrT.inc(t.matmul(p_m2[:], s_ones1[:], s_t11[:],
                                  start=True, stop=True))
            for kind, nt in sched:
                if kind == "o":
                    for half in range(2):
                        pb = p_o[(nt * 2 + half) % 4]
                        grp = nt * 2 + half - 4
                        if grp >= 0:
                            t.wait_ge(ve.h, ve4_0 + (grp // 2) * VE_PER + (grp % 2) + 1)
                        for kc in range(NCH):
                            mm = t.matmul(pb[:],
                                          s_aor[:, kc * S + nt * 128:kc * S + (nt + 1) * 128],
                                          s_wout[:, kc * D + half * 512: kc * D + (half + 1) * 512],
                                          start=(kc == 0), stop=(kc == NCH - 1))
                        te.inc(mm)
                else:
                    t.wait_ge(ve.h, ve4_0 + nt * VE_PER + 20)
                    for b in range(2):
                        pb = p_tr[b]
                        if nt > 0:
                            t.wait_ge(ve.h, ve4_0 + (nt - 1) * VE_PER + 20 + b + 1)
                        for kk in range(4):
                            kc = b * 4 + kk
                            mm = t.transpose(pb[:, kk * 128:(kk + 1) * 128],
                                             s_xqtm[:, kc * 128:(kc + 1) * 128],
                                             s_ident[:])
                        te.inc(mm)
        te_tr_end = te.n
        assert te_tr_end == trun4 == te4_0 + 4 * NT

        @block.vector
        def _(v):
            v.wait_ge(tpA.h, NWCH)
            for mi6 in range(3):
                v.drain()
                thrV.inc(v.tensor_reduce(s_red[:, mi6:mi6 + 1],
                                         s_msum[:, mi6 * 32:(mi6 + 1) * 32],
                                         AXX, ALU.add))
            for mi6 in range(3):
                v.wait_ge(thrT.h, 2 * mi6 + 1)
                v.drain()
                thrV.inc(v.tensor_copy(s_t11[:], p_m1[:]))
                v.wait_ge(thrT.h, 2 * mi6 + 2)
                thrV.inc(v.tensor_copy(s_thr[:, mi6:mi6 + 1], p_m2[:]))
            v.drain()
            thrV.inc(v.tensor_scalar(s_thr[:, 0:3], s_thr[:, 0:3],
                                     CC(1.0 / NW), CC(1e-5), ALU.mult, ALU.max))
            v.drain()
            thrV.inc(v.tensor_scalar(s_thr[:, 0:3], s_thr[:, 0:3],
                                     CC(0.5), CC(1.0), ALU.mult, ALU.mult))
            v.drain()
            thrV.inc(v.tensor_scalar(s_thr[:, 4:7], s_thr[:, 0:3],
                                     CC(-1.0), CC(0.0), ALU.mult, ALU.add))
            v.drain()

            for nt in range(NT):
                v.wait_ge(te.h, te_evt[("o", nt, 1)])
                v.wait_ge(dX[dS_xland[nt][0]].h, dS_xland[nt][1])
                for half in range(2):
                    pb = p_o[(nt * 2 + half) % 4]
                    ve.inc(v.tensor_tensor(s_y1[:, half * 512:(half + 1) * 512],
                                           s_xt[nt % 2][:, half * 512:(half + 1) * 512],
                                           pb[:], ALU.add))
                v.drain()
                ve.inc(v.tensor_scalar(s_r1[:], s_y1[:], CC(1.0), CC(0.0),
                                       ALU.mult, ALU.add, accum_out=stc("sum")))
                v.drain()
                v.wait_ge(se.h, se4_0 + nt * SE_PER + 1)
                ve.inc(v.tensor_scalar(stc("mean"), stc("sum"), CC(1.0 / D), CC(0.0),
                                       ALU.mult, ALU.add))
                v.drain()
                ve.inc(v.tensor_tensor(stc("msq"), stc("mean"), stc("mean"), ALU.mult))
                v.drain()
                ve.inc(v.tensor_scalar(stc("var"), stc("sumsq"), CC(1.0 / D), CC(0.0),
                                       ALU.mult, ALU.add))
                v.drain()
                ve.inc(v.tensor_tensor(stc("var"), stc("var"), stc("msq"), ALU.subtract))
                v.drain()
                ve.inc(v.tensor_scalar(stc("var"), stc("var"), CC(1e-5), CC(1.0),
                                       ALU.add, ALU.mult))
                v.drain()
                ve.inc(v.reciprocal(stc("rec"), stc("var")))
                v.wait_ge(se.h, se4_0 + nt * SE_PER + 2)
                ve.inc(v.tensor_tensor(stc("negmr"), stc("mean"), stc("rstd"), ALU.mult))
                v.drain()
                ve.inc(v.tensor_scalar(stc("negmr"), stc("negmr"), CC(-1.0), CC(0.0),
                                       ALU.mult, ALU.add))
                v.wait_ge(se.h, se4_0 + nt * SE_PER + 3)
                ve.inc(v.tensor_reduce(stc("amax"), s_y1n[nt % 2][:], AXX, ALU.max,
                                       apply_absolute_value=True))
                v.drain()
                ve.inc(v.tensor_scalar(s_amx[:, nt:nt + 1], stc("amax"), CC(1e-5), CC(1.0),
                                       ALU.max, ALU.mult))
                v.drain()
                ve.inc(v.reciprocal(stc("rec2"), s_amx[:, nt:nt + 1]))
                v.drain()
                ve.inc(v.tensor_scalar(stc("s1"), stc("rec2"), CC(127.0), CC(0.0),
                                       ALU.mult, ALU.add))
                v.drain()
                ve.inc(v.tensor_scalar(s_z[:], s_y1n[nt % 2][:], stc("s1"), CC(512.0),
                                       ALU.mult, ALU.add))
                v.drain()
                ve.inc(v.tensor_copy(s_i32[:], s_z[:]))
                v.drain()
                ve.inc(v.tensor_copy(s_r1[:], s_i32[:]))
                v.drain()
                ve.inc(v.scalar_tensor_tensor(s_u[:], s_z[:], -0.5, s_r1[:],
                                              ALU.add, ALU.is_ge))
                v.drain()
                ve.inc(v.scalar_tensor_tensor(s_xqtm[:], s_r1[:], -512.0, s_u[:],
                                              ALU.add, ALU.add))
                for b in range(2):
                    v.wait_ge(te.h, te_evt[("t", nt, b)])
                    if nt >= 2:
                        v.wait_ge(dXq[dS_xq[nt - 2][0]].h, dS_xq[nt - 2][1])
                    ve.inc(v.tensor_copy(s_xqst[nt % 2][:, b * 512:(b + 1) * 512],
                                         p_tr[b][:]))
        ve_p4 = ve.n
        assert ve_p4 == ve4_0 + NT * VE_PER, (ve_p4, ve4_0)

        tern_emitS = [0]

        @block.scalar
        def _(s):
            def _signs(s, hi):
                while tern_emitS[0] < hi:
                    i6 = tern_emitS[0]
                    if i6 == 0:
                        s.wait_ge(thrV.h, 12)
                    s.wait_ge(dW5[(NWCH + i6) % 3].h, pvW(NWCH + i6))
                    if i6 >= 2:
                        s.wait_ge(dTst[(i6 - 2) % 4].h,
                                  (((i6 - 2) // 4) + 1) * 16)
                    mi6 = WCHUNKS[i6][0]
                    ts2.inc(s.activation(s_ta[i6 % 2][:], s_wst[(NWCH + i6) % 3][:],
                                         AF.Sign,
                                         bias=s_thr[:, 4 + mi6:5 + mi6],
                                         scale=1.0))
                    ts2.inc(s.activation(s_tb[i6 % 2][:], s_wst[(NWCH + i6) % 3][:],
                                         AF.Sign,
                                         bias=s_thr[:, mi6:mi6 + 1],
                                         scale=1.0))
                    tern_emitS[0] += 1
            for nt in range(NT):
                s.wait_ge(ve.h, ve4_0 + nt * VE_PER + 2)
                se.inc(s.activation(s_tr2[:], s_y1[:], AF.Square, accum_out=stc("sumsq")))
                s.wait_ge(ve.h, ve4_0 + nt * VE_PER + 9)
                se.inc(s.activation(stc("rstd"), stc("rec"), AF.Sqrt))
                s.drain()
                s.wait_ge(ve.h, ve4_0 + nt * VE_PER + 11)
                if nt >= 2:
                    s.wait_ge(dY1n[dS_y1n[nt - 2][0]].h, dS_y1n[nt - 2][1])
                se.inc(s.activation(s_y1n[nt % 2][:], s_y1[:], AF.Identity,
                                    bias=stc("negmr"), scale=stc("rstd")))
                _signs(s, ((nt + 1) * NWCH) // NT)
            _signs(s, NWCH)
        se_p4 = se.n
        assert se_p4 == se4_0 + NT * SE_PER

        if "sttap" in dbg:
            @block.sync
            def _(sy):
                sy.wait_ge(ve.h, ve_p4)
                sy.wait_ge(se.h, se_p4)
                dS.dinc(sy.dma_start(out=dbg["sttap"], in_=s_st[:]))
        if "y1n" in dbg:
            @block.sync
            def _(sy):
                sy.wait_ge(dY1n[0].h, dS_p4[0])
                sy.wait_ge(dY1n[1].h, dS_p4[1])
                dS.dinc(sy.dma_start(out=dbg["y1n"], in_=d_y1s))
        if "xq" in dbg:
            @block.sync
            def _(sy):
                sy.wait_ge(dXq[0].h, dS_p4[2])
                sy.wait_ge(dXq[1].h, dS_p4[3])
                dS.dinc(sy.dma_start(out=dbg["xq"], in_=d_xqs))

        if stop_phase <= 4:
            _finish(block, ctrs)
            return nc

        ve_p5c = ve.n
        se_p5a = se.n
        dO_p5 = (dTst[0].n, dTst[1].n, dTst[2].n, dTst[3].n)
        tern_totals = [(tpA.h, tpA.n), (thrV.h, thrV.n), (thrT.h, thrT.n),
                       (ts2.h, ts2.n), (tv2.h, tv2.n),
                       (dAcc[0].h, dAcc[0].n), (dAcc[1].h, dAcc[1].n),
                       (dW5[0].h, dW5[0].n), (dW5[1].h, dW5[1].n),
                       (dW5[2].h, dW5[2].n),
                       (dY1n[0].h, dY1n[0].n), (dY1n[1].h, dY1n[1].n),
                       (dXq[0].h, dXq[0].n), (dXq[1].h, dXq[1].n),
                       (dX[0].h, dX[0].n), (dX[1].h, dX[1].n),
                       (dAor.h, dAor.n), (dDen[0].h, dDen[0].n),
                       (dDen[1].h, dDen[1].n), (dWo.h, dWo.n),
                       (dRst[0].h, dRst[0].n), (dRst[1].h, dRst[1].n),
                       (dO.h, dO.n), (dS.h, dS.n), (dC2.h, dC2.n),
                       (dG.h, dG.n)]

        if stop_phase <= 5:
            _finish(block, ctrs)
            return nc

        te_p5 = te.n
        scD.close()
        scF.close()
        psC.close()
        scH = _scope()
        s_xql = [mk(scH, f"h_xql{i}", [128, NCH * 512], BF16) for i in range(2)]
        s_tg = [mk(scH, f"h_tg{i}", [128, NCH * 512], BF16) for i in range(2)]
        s_tu = [mk(scH, f"h_tu{i}", [128, NCH * 512], BF16) for i in range(2)]
        s_h = [mk(scH, f"h_h{i}", [128, INNER]) for i in range(NTS)]
        s_sg = [mk(scH, f"h_sg{i}", [128, 512], BF16) for i in range(2)]
        s_z2 = mk(scH, "h_z2", [128, 1024])
        s_i2 = mk(scH, "h_i2", [128, 1024], I16)
        s_u2 = mk(scH, "h_u2", [128, 1024], BF16)
        s_hq = [mk(scH, f"h_hq{i}", [128, INNER], BF16) for i in range(3)]
        s_hqT = [mk(scH, f"h_hqT{i}", [128, 32 * 128], BF16) for i in range(NTS)]
        s_td = [mk(scH, f"h_td{i}", [128, 2 * 512], BF16) for i in range(2)]
        s_y1t = [mk(scH, f"h_y1t{i}", [128, D]) for i in range(2)]
        s_os = [mk(scH, f"h_os{i}", [128, D]) for i in range(NTS)]
        s_st2 = mk(scH, "h_st2", [128, 24])
        s_sc2 = mk(scH, "h_sc2", [128, 2 * NTS * 5])

        psF = _scope()
        p_a = [psF.enter_context(nc.psum_tensor(f"p6a{i}", [128, 512], F32))
               for i in range(4)]
        p_t = [psF.enter_context(nc.psum_tensor(f"p6t{i}", [128, 512], BF16))
               for i in range(4)]

        ST2 = {k: 12 + i for i, k in enumerate(
            ["mean", "msq", "var", "rec", "rstd", "negmr", "ac", "t1", "amt"])}

        def st2(name):
            return s_st2[:, ST2[name]:ST2[name] + 1]

        def stk(name, k):
            i = {"sum": 0, "sqa": 4, "sqb": 8}[name] + k
            return s_st2[:, i:i + 1]

        SC2W = {"amx": 0, "gsc": 1, "usc": 2, "zq": 3, "dsc": 4}

        def sc2c(j, k, w):
            i = (j % 2) * NTS * 5 + k * 5 + SC2W[w]
            return s_sc2[:, i:i + 1]

        def hqb(j, k):
            return s_hq[(j * NTS + k) % 3]

        te_all6 = {}
        tn6 = te.n
        for j in range(NSUP):
            for o in range(NCH):
                for k in range(NTS):
                    tn6 += 1
                    te_all6[("gu", j, o * NTS + k, "g")] = tn6
                    tn6 += 1
                    te_all6[("gu", j, o * NTS + k, "u")] = tn6
            if j >= 1:
                for h2 in range(2):
                    for g2 in range(16):
                        tn6 += 1
                        te_all6[("dn", j - 1, h2, g2)] = tn6
            for k in range(NTS):
                for g in range(8):
                    tn6 += 1
                    te_all6[("tr", j, k, g)] = tn6
        for h2 in range(2):
            for g2 in range(16):
                tn6 += 1
                te_all6[("dn", NSUP - 1, h2, g2)] = tn6
        te6_end = tn6

        def pv(i):
            return ((i // 2) + 1) * 16

        def pv2(i):
            return ((i // 2) + 1) * 32

        dOut0 = dOut.n

        ve_ops, se_ops = [], []

        def V(key, fn, waits=(), drain=False):
            ve_ops.append((key, fn, waits, drain))

        def SEO(key, fn, waits=(), drain=False):
            se_ops.append((key, fn, waits, drain))

        def qhead(j, k):
            V(None, lambda v, j=j, k=k: v.tensor_tensor(
                st2("ac"), sc2c(j, k, "amx"), sc2c(j, k, "usc"), ALU.mult),
                drain=True)
            V(None, lambda v: v.tensor_scalar(
                st2("ac"), st2("ac"), CC(1e-5), CC(1.0), ALU.max, ALU.mult),
                drain=True)
            V(None, lambda v: v.reciprocal(st2("rec"), st2("ac")), drain=True)
            V(None, lambda v, j=j, k=k: v.tensor_tensor(
                st2("t1"), sc2c(j, k, "usc"), st2("rec"), ALU.mult),
                drain=True)
            V(None, lambda v, j=j, k=k: v.tensor_scalar(
                sc2c(j, k, "zq"), st2("t1"), CC(127.0), CC(0.0),
                ALU.mult, ALU.add), drain=True)
            V(None, lambda v, j=j, k=k: v.tensor_tensor(
                st2("msq"), st2("ac"), s_thr[:, 2:3], ALU.mult), drain=True)
            V(("qh", j, k), lambda v, j=j, k=k: v.tensor_scalar(
                sc2c(j, k, "dsc"), st2("msq"), CC(1.0 / 127.0), CC(0.0),
                ALU.mult, ALU.add), drain=True)

        def qchunks(j, k):
            gt = j * NTS + k
            for c in range(4):
                cols = slice(c * 1024, (c + 1) * 1024)
                w = []
                if c == 0 and gt >= 3:
                    pj, pk = divmod(gt - 3, NTS)
                    w.append(("te", ("tr", pj, pk, 7)))
                V(("z2", j, k, c), lambda v, j=j, k=k, c=c, cols=cols:
                    v.tensor_scalar(s_z2[:], s_h[k][:, cols],
                                    sc2c(j, k, "zq"), CC(512.0),
                                    ALU.mult, ALU.add),
                    waits=w, drain=True)
                V(None, lambda v: v.tensor_copy(s_i2[:], s_z2[:]),
                    drain=True)
                V(None, lambda v: v.scalar_tensor_tensor(
                    s_u2[:], s_z2[:], -0.5, s_i2[:], ALU.add, ALU.is_ge),
                    drain=True)
                V(("hq", j, k, c), lambda v, j=j, k=k, c=c, cols=cols:
                    v.scalar_tensor_tensor(hqb(j, k)[:, cols], s_i2[:],
                                           -512.0, s_u2[:], ALU.add,
                                           ALU.add), drain=True)

        def osbatch(jd, h2):
            for k in range(NTS):
                w = []
                if k == 0:
                    w.append(("te", ("dn", jd, h2, 15)))
                if h2 == 0 and jd >= 1:
                    w.append(("dma", dOut, dOut0 + jd * NTS * 16))
                half = slice(h2 * 512, (h2 + 1) * 512)
                V(("osh" + str(h2), jd, k), lambda v, jd=jd, k=k, half=half:
                    v.tensor_scalar(s_os[k][:, half], p_a[k][:],
                                    sc2c(jd, k, "dsc"), CC(0.0),
                                    ALU.mult, ALU.add), waits=w)

        def chainA(jd, k):
            gt = jd * NTS + k
            V(("y2", jd, k), lambda v, jd=jd, k=k: v.tensor_tensor(
                s_os[k][:], s_os[k][:], s_y1t[k % 2][:], ALU.add),
                waits=[("dma", dY1l[gt % 2], pv(gt))], drain=True)
            V(("y2s", jd, k), lambda v, k=k: v.tensor_scalar(
                s_os[k][:], s_os[k][:], CC(1.0), CC(0.0), ALU.mult, ALU.add,
                accum_out=stk("sum", k)), drain=True)

        def chainB(jd, k):
            V(None, lambda v, k=k: v.tensor_scalar(
                st2("mean"), stk("sum", k), CC(1.0 / 1024), CC(0.0),
                ALU.mult, ALU.add), drain=True)
            V(None, lambda v: v.tensor_tensor(
                st2("msq"), st2("mean"), st2("mean"), ALU.mult), drain=True)
            V(None, lambda v, k=k: v.tensor_tensor(
                st2("var"), stk("sqa", k), stk("sqb", k), ALU.add),
                waits=[("se", ("sqb", jd, k))], drain=True)
            V(None, lambda v: v.scalar_tensor_tensor(
                st2("var"), st2("var"), 1.0 / 1024, st2("msq"),
                ALU.mult, ALU.subtract), drain=True)
            V(None, lambda v: v.tensor_scalar(
                st2("var"), st2("var"), CC(1e-5), CC(1.0), ALU.add,
                ALU.mult), drain=True)
            V(("rec", jd, k), lambda v: v.reciprocal(st2("rec"), st2("var")),
                drain=True)
            V(None, lambda v: v.tensor_tensor(
                st2("negmr"), st2("mean"), st2("rstd"), ALU.mult),
                waits=[("se", ("rstd", jd, k))], drain=True)
            V(None, lambda v: v.tensor_scalar(
                st2("negmr"), st2("negmr"), CC(-1.0), CC(0.0), ALU.mult,
                ALU.add), drain=True)
            V(("out", jd, k), lambda v, k=k: v.tensor_scalar(
                s_os[k][:], s_os[k][:], st2("rstd"), st2("negmr"),
                ALU.mult, ALU.add), drain=True)

        def evacs(j, lo, hi):
            for tg_i in range(lo, hi):
                k, g = divmod(tg_i, 8)
                V(("evac", j, tg_i), lambda v, j=j, k=k, g=g, tg_i=tg_i:
                    v.tensor_copy(s_hqT[k][:, g * 512:(g + 1) * 512],
                                  p_t[tg_i % 4][:]),
                    waits=[("te", ("tr", j, k, g))])

        for j in range(NSUP):
            for k in range(NTS):
                nt = j * NTS + k
                V(None, lambda v, j=j, k=k, nt=nt: v.tensor_tensor(
                    sc2c(j, k, "gsc"), s_amx[:, nt:nt + 1], s_thr[:, 0:1],
                    ALU.mult), drain=True)
                V(None, lambda v, j=j, k=k: v.tensor_scalar(
                    sc2c(j, k, "gsc"), sc2c(j, k, "gsc"), CC(1.0 / 127.0),
                    CC(0.0), ALU.mult, ALU.add), drain=True)
                V(None, lambda v, j=j, k=k, nt=nt: v.tensor_tensor(
                    sc2c(j, k, "usc"), s_amx[:, nt:nt + 1], s_thr[:, 1:2],
                    ALU.mult), drain=True)
                V(("prep", j, k), lambda v, j=j, k=k: v.tensor_scalar(
                    sc2c(j, k, "usc"), sc2c(j, k, "usc"), CC(1.0 / 127.0),
                    CC(0.0), ALU.mult, ALU.add), drain=True)
            for gi in range(32):
                o, k = divmod(gi, NTS)
                V(("hTT", j, gi), lambda v, j=j, gi=gi, o=o, k=k: v.tensor_tensor(
                    s_h[k][:, o * 512:(o + 1) * 512], s_sg[gi % 2][:],
                    p_a[1 if gi % 2 == 0 else 3][:], ALU.mult),
                    waits=[("te", ("gu", j, gi, "u")), ("se", ("silu", j, gi))])
                sl = slice(o * 512, (o + 1) * 512)
                if o == 0:
                    V(None, lambda v, j=j, k=k, sl=sl: v.tensor_reduce(
                        sc2c(j, k, "amx"), s_h[k][:, sl], AXX, ALU.max,
                        apply_absolute_value=True), drain=True)
                else:
                    V(None, lambda v, j=j, k=k, sl=sl: v.tensor_reduce(
                        st2("amt"), s_h[k][:, sl], AXX, ALU.max,
                        apply_absolute_value=True), drain=True)
                    key = ("amax", j, k) if o == NCH - 1 else None
                    V(key, lambda v, j=j, k=k: v.tensor_tensor(
                        sc2c(j, k, "amx"), sc2c(j, k, "amx"), st2("amt"),
                        ALU.max), drain=True)
                if o == NCH - 1:
                    qhead(j, k)

            qchunks(j, 0)
            if j >= 1:
                osbatch(j - 1, 0)
            qchunks(j, 1)
            qchunks(j, 2)
            evacs(j, 0, 8)
            qchunks(j, 3)
            evacs(j, 8, 16)
            if j >= 1:
                osbatch(j - 1, 1)
            evacs(j, 16, 24)
            if j >= 1:
                chainA(j - 1, 0)
                chainA(j - 1, 1)
            evacs(j, 24, 32)
            if j >= 1:
                chainA(j - 1, 2)
                chainA(j - 1, 3)
                for k in range(NTS):
                    chainB(j - 1, k)
        osbatch(NSUP - 1, 0)
        osbatch(NSUP - 1, 1)
        for k in range(NTS):
            chainA(NSUP - 1, k)
        for k in range(NTS):
            chainB(NSUP - 1, k)

        for j in range(NSUP):
            for gi in range(32):
                o, k = divmod(gi, NTS)
                w = [("te", ("gu", j, gi, "g"))]
                if gi == 0:
                    w.append(("ve", ("prep", j, NTS - 1)))
                if gi >= 2:
                    w.append(("ve", ("hTT", j, gi - 2)))
                elif j >= 1:
                    w.append(("ve", ("hTT", j - 1, 30 + gi)))
                SEO(("silu", j, gi), lambda s, j=j, gi=gi, k=k: s.activation(
                    s_sg[gi % 2][:], p_a[0 if gi % 2 == 0 else 2][:], AF.Silu,
                    scale=sc2c(j, k, "gsc")), waits=w,
                    drain=(gi <= 1))

            def ln2sq(jd, k):
                SEO(None, lambda s, k=k: s.activation(
                    s_sg[0][:], s_os[k][:, 0:512], AF.Square,
                    accum_out=stk("sqa", k)),
                    waits=[("ve", ("y2s", jd, k))], drain=True)
                SEO(("sqb", jd, k), lambda s, k=k: s.activation(
                    s_sg[1][:], s_os[k][:, 512:1024], AF.Square,
                    accum_out=stk("sqb", k)), drain=True)

            def ln2rt(jd, k):
                SEO(("rstd", jd, k), lambda s: s.activation(
                    st2("rstd"), st2("rec"), AF.Sqrt),
                    waits=[("ve", ("rec", jd, k))], drain=True)

            if j >= 1:
                for k in range(NTS):
                    ln2sq(j - 1, k)
                for k in range(NTS):
                    ln2rt(j - 1, k)
        for k in range(NTS):
            ln2sq(NSUP - 1, k)
        for k in range(NTS):
            ln2rt(NSUP - 1, k)

        ve6_0, se6_0 = ve.n, se.n
        ve_evt = {}
        for i, (key, _, _, _) in enumerate(ve_ops):
            if key is not None:
                ve_evt[key] = ve6_0 + i + 1
        se_evt = {}
        for i, (key, _, _, _) in enumerate(se_ops):
            if key is not None:
                se_evt[key] = se6_0 + i + 1

        def _w(eng, w):
            if w[0] == "te":
                eng.wait_ge(te.h, te_all6[w[1]])
            elif w[0] == "ve":
                eng.wait_ge(ve.h, ve_evt[w[1]])
            elif w[0] == "se":
                eng.wait_ge(se.h, se_evt[w[1]])
            elif w[0] == "dma":
                eng.wait_ge(w[1].h, w[2])

        def _xql_load(sy, j):
            inap = bass.AP(d_xqs.tensor, j * 512,
                           [[S * 128, NCH], [S, 128], [1, 512]])
            outap = bass.AP(s_xql[j % 2][:].tensor, s_xql[j % 2][:].offset,
                            [[NCH * 512, 128], [512, NCH], [1, 512]])
            dXql[j % 2].dinc(sy.dma_start(out=outap, in_=inap))
            assert dXql[j % 2].n == pv(j)

        def _slice_load(sy, j, o):
            gs = j * NCH + o
            for dsrc2, buf in ((d_terng, s_tg), (d_ternu, s_tu)):
                inap = bass.AP(dsrc2.tensor, o * 512,
                               [[INNER * 128, NCH], [INNER, 128], [1, 512]])
                outap = bass.AP(buf[gs % 2][:].tensor, buf[gs % 2][:].offset,
                                [[NCH * 512, 128], [512, NCH], [1, 512]])
                dGu[gs % 2].dinc(sy.dma_start(out=outap, in_=inap))
            assert dGu[gs % 2].n == pv2(gs)

        @block.sync
        def _(sy):
            sy.wait_ge(ve.h, ve_p5c)
            sy.wait_ge(se.h, se_p5a)
            sy.wait_ge(te.h, te_p5)
            for _i6 in range(4):
                sy.wait_ge(dTst[_i6].h, dO_p5[_i6])
            for _h6, _v6 in tern_totals:
                if _v6:
                    sy.wait_ge(_h6, _v6)
            _xql_load(sy, 0)
            _xql_load(sy, 1)
            _slice_load(sy, 0, 0)
            _slice_load(sy, 0, 1)
            for j in range(NSUP):
                for o in range(2, NCH):
                    sy.wait_ge(te.h, te_all6[("gu", j, (o - 2) * NTS + NTS - 1, "u")])
                    _slice_load(sy, j, o)
                if j + 2 < NSUP:
                    sy.wait_ge(te.h, te_all6[("gu", j, 31, "u")])
                    _xql_load(sy, j + 2)
                if j + 1 < NSUP:
                    sy.wait_ge(te.h, te_all6[("gu", j, 27, "u")])
                    _slice_load(sy, j + 1, 0)
                    sy.wait_ge(te.h, te_all6[("gu", j, 31, "u")])
                    _slice_load(sy, j + 1, 1)
                if j >= 1:
                    for k in range(NTS):
                        _w(sy, ("ve", ("out", j - 1, k)))
                        nt = (j - 1) * NTS + k
                        dOut.dinc(sy.dma_start(
                            out=d_out[nt * 128:(nt + 1) * 128, :],
                            in_=s_os[k][:]))
            for k in range(NTS):
                _w(sy, ("ve", ("out", NSUP - 1, k)))
                nt = (NSUP - 1) * NTS + k
                dOut.dinc(sy.dma_start(
                    out=d_out[nt * 128:(nt + 1) * 128, :], in_=s_os[k][:]))

        @block.gpsimd
        def _(g):
            g.wait_ge(ve.h, ve_p5c)
            g.wait_ge(se.h, se_p5a)
            g.wait_ge(te.h, te_p5)
            for _i6 in range(4):
                g.wait_ge(dTst[_i6].h, dO_p5[_i6])
            for _h6, _v6 in tern_totals:
                if _v6:
                    g.wait_ge(_h6, _v6)
            g.wait_ge(dY1n[0].h, dS_p4[0])
            g.wait_ge(dY1n[1].h, dS_p4[1])

            def _td(jd, h2, g2):
                dn = jd * 32 + h2 * 16 + g2
                if dn >= 2:
                    pdn = dn - 2
                    g.wait_ge(te.h, te_all6[("dn", pdn // 32,
                                             (pdn % 32) // 16, pdn % 16)])
                inap = bass.AP(d_ternd.tensor, g2 * 256 * D + h2 * 512,
                               [[D, 128], [128 * D, 2], [1, 512]])
                buf = s_td[dn % 2]
                outap = bass.AP(buf[:].tensor, buf[:].offset,
                                [[2 * 512, 128], [512, 2], [1, 512]])
                dTd[dn % 2].dinc(g.dma_start(out=outap, in_=inap))
                assert dTd[dn % 2].n == pv(dn)

            def _y1(jd, k):
                gt = jd * NTS + k
                if gt >= 2:
                    g.wait_ge(ve.h, ve_evt[("y2", (gt - 2) // NTS,
                                            (gt - 2) % NTS)])
                dY1l[gt % 2].dinc(g.dma_start(
                    out=s_y1t[k % 2][:], in_=d_y1s[gt * 128:(gt + 1) * 128, :]))
                assert dY1l[gt % 2].n == pv(gt)

            for jd in range(NSUP):
                for h2 in range(2):
                    for g2 in range(16):
                        _td(jd, h2, g2)
                for k in range(NTS):
                    _y1(jd, k)

        @block.tensor
        def _(t):
            def emit_down(jd, final):
                for h2 in range(2):
                    for g2 in range(16):
                        dn = jd * 32 + h2 * 16 + g2
                        t.wait_ge(dTd[dn % 2].h, pv(dn))
                        if h2 == 0 and g2 == 0:
                            t.wait_ge(ve.h, ve_evt[("evac", jd, 31)])
                        for kk in range(2):
                            for k in range(NTS):
                                if kk == 0 and g2 == 0:
                                    if h2 == 1:
                                        t.wait_ge(ve.h, ve_evt[("osh0", jd, k)])
                                    elif final:
                                        t.wait_ge(ve.h, ve_evt[("osh1", jd - 1, k)])
                                    else:
                                        jn = jd + 1
                                        if k == 0:
                                            t.wait_ge(se.h, se_evt[("silu", jn, 30)])
                                        elif k == 1:
                                            t.wait_ge(ve.h, ve_evt[("hTT", jn, 30)])
                                        elif k == 2:
                                            t.wait_ge(se.h, se_evt[("silu", jn, 31)])
                                        else:
                                            t.wait_ge(ve.h, ve_evt[("hTT", jn, 31)])
                                kc = g2 * 2 + kk
                                mm = t.matmul(
                                    p_a[k][:],
                                    s_hqT[k][:, kc * 128:(kc + 1) * 128],
                                    s_td[dn % 2][:, kk * 512:(kk + 1) * 512],
                                    start=(g2 == 0 and kk == 0),
                                    stop=(g2 == 15 and kk == 1))
                        te.inc(mm)
                        assert te.n == te_all6[("dn", jd, h2, g2)]

            for j in range(NSUP):
                for o in range(NCH):
                    gs = j * NCH + o
                    t.wait_ge(dGu[gs % 2].h, pv2(gs))
                    if o == 0:
                        t.wait_ge(dXql[j % 2].h, pv(j))
                    for k in range(NTS):
                        gi = o * NTS + k
                        pg = p_a[0 if gi % 2 == 0 else 2]
                        pu = p_a[1 if gi % 2 == 0 else 3]
                        if gi >= 2:
                            t.wait_ge(se.h, se_evt[("silu", j, gi - 2)])
                        elif j >= 2:
                            t.wait_ge(ve.h, ve_evt[("osh1", j - 2, 0 if gi == 0 else 2)])
                        elif j == 1:
                            t.wait_ge(se.h, se_evt[("silu", 0, 30 + gi)])
                        for kc in range(NCH):
                            mm = t.matmul(
                                pg[:],
                                s_xql[j % 2][:, kc * 512 + k * 128:kc * 512 + (k + 1) * 128],
                                s_tg[gs % 2][:, kc * 512:(kc + 1) * 512],
                                start=(kc == 0), stop=(kc == NCH - 1))
                        te.inc(mm)
                        assert te.n == te_all6[("gu", j, gi, "g")]
                        if gi >= 2:
                            t.wait_ge(ve.h, ve_evt[("hTT", j, gi - 2)])
                        elif j >= 2:
                            t.wait_ge(ve.h, ve_evt[("osh1", j - 2, 1 if gi == 0 else 3)])
                        elif j == 1:
                            t.wait_ge(ve.h, ve_evt[("hTT", 0, 30 + gi)])
                        for kc in range(NCH):
                            mm = t.matmul(
                                pu[:],
                                s_xql[j % 2][:, kc * 512 + k * 128:kc * 512 + (k + 1) * 128],
                                s_tu[gs % 2][:, kc * 512:(kc + 1) * 512],
                                start=(kc == 0), stop=(kc == NCH - 1))
                        te.inc(mm)
                        assert te.n == te_all6[("gu", j, gi, "u")]
                if j >= 1:
                    emit_down(j - 1, final=False)
                for k in range(NTS):
                    for g in range(8):
                        tg_i = k * 8 + g
                        if g == 0:
                            t.wait_ge(ve.h, ve_evt[("hq", j, k, 3)])
                        if tg_i >= 4:
                            t.wait_ge(ve.h, ve_evt[("evac", j, tg_i - 4)])
                        elif j >= 1:
                            t.wait_ge(ve.h, ve_evt[("evac", j - 1, 28 + tg_i)])
                        pb = p_t[tg_i % 4]
                        for kk in range(4):
                            kc = g * 4 + kk
                            mm = t.transpose(pb[:, kk * 128:(kk + 1) * 128],
                                             hqb(j, k)[:, kc * 128:(kc + 1) * 128],
                                             s_ident[:])
                        te.inc(mm)
                        assert te.n == te_all6[("tr", j, k, g)]
            emit_down(NSUP - 1, final=True)
            assert te.n == te6_end

        @block.vector
        def _(v):
            for key, fn, waits, drain in ve_ops:
                for w in waits:
                    _w(v, w)
                if drain:
                    v.drain()
                ve.inc(fn(v))
        assert ve.n == ve6_0 + len(ve_ops)

        @block.scalar
        def _(s):
            for key, fn, waits, drain in se_ops:
                for w in waits:
                    _w(s, w)
                if drain:
                    s.drain()
                se.inc(fn(s))
        assert se.n == se6_0 + len(se_ops)

        _finish(block, ctrs)
        return nc

def _finish(block, ctrs):
    @block.sync
    def _(sy):
        for c in ctrs:
            if c.n:
                sy.wait_ge(c.h, c.n)



def _host_constants():
    cons = np.tile(np.array(CONST_VALS, dtype=np.float32), (128, 1))
    ident = np.eye(128, dtype=ml_dtypes.bfloat16)
    inv_freq = (1.0 / (10000.0 ** (np.arange(0, DH, 2, dtype=np.float32) / DH))).astype(np.float32)
    t = np.arange(S, dtype=np.float32)
    freqs = t[None, :] * inv_freq[:, None]
    cos32 = np.cos(freqs).astype(np.float32)
    sin32 = np.sin(freqs).astype(np.float32)
    cosf = np.tile(cos32, (4, 1))
    sinf = np.concatenate([-sin32, sin32, -sin32, sin32], axis=0)
    kl = np.arange(128)[:, None]
    ql = np.arange(512)[None, :]
    masks = np.concatenate(
        [np.where(128 * r + kl > ql, np.float32(-1e38), np.float32(0.0))
         for r in range(4)], axis=1).astype(ml_dtypes.bfloat16)
    sel2 = np.zeros((2, 128), dtype=np.float32)
    sel2[0, 0:64] = 1.0
    sel2[1, 64:128] = 1.0
    ones1 = np.ones((1, 128), dtype=np.float32)
    return cons, ident, np.ascontiguousarray(cosf), np.ascontiguousarray(sinf), masks, sel2, ones1


def kernel(L0_attn_x, L0_attn_w_qkv, L0_attn_w_out, L0_ffn_w_gate, L0_ffn_w_up,
           L0_ffn_w_down):
    x = np.ascontiguousarray(np.asarray(L0_attn_x, dtype=np.float32))
    wqkv = np.ascontiguousarray(np.asarray(L0_attn_w_qkv, dtype=np.float32))
    wout = np.ascontiguousarray(np.asarray(L0_attn_w_out, dtype=np.float32))
    wgT = np.ascontiguousarray(np.asarray(L0_ffn_w_gate, dtype=np.float32).T)
    wuT = np.ascontiguousarray(np.asarray(L0_ffn_w_up, dtype=np.float32).T)
    wdT = np.ascontiguousarray(np.asarray(L0_ffn_w_down, dtype=np.float32).T)
    cons, ident, cosf, sinf, masks, sel2, ones1 = _host_constants()

    nc = build_program()
    in_maps = []
    for b in range(8):
        xb = np.ascontiguousarray(x[b])
        in_maps.append({
            "xT": np.ascontiguousarray(xb.T), "x": xb,
            "wqkv": wqkv, "wout": wout, "wgT": wgT, "wuT": wuT, "wdT": wdT,
            "consts": cons, "cosf": cosf, "sinf": sinf, "ident": ident,
            "masks": masks, "sel2": sel2, "ones1": ones1,
        })
    res = run_bass_kernel_spmd(nc, in_maps, core_ids=list(range(8)))
    return np.stack([res.results[b]["out"] for b in range(8)], axis=0)


# version 09
# speedup vs baseline: 1.1053x; 1.1053x over previous
import contextlib
import numpy as np
import ml_dtypes
import concourse.bass as bass
import concourse.mybir as mybir
from concourse.bass_utils import run_bass_kernel_spmd

F32 = mybir.dt.float32
BF16 = mybir.dt.bfloat16
I32 = mybir.dt.int32
I16 = mybir.dt.int16
AF = mybir.ActivationFunctionType
ALU = mybir.AluOpType
AXX = mybir.AxisListType.X

S = 2048
D = 1024
H = 16
DH = 64
NCH = 8
INNER = 4096
NT = 16
NJ = 4
CSTAB = 25.0
NW = INNER * D
NSUP = 4
NTS = NT // NSUP

CONST_VALS = [0.0, 1.0, -1.0, 0.5, -0.5, 0.125, -CSTAB, 512.0, -512.0,
              1.0 / D, 1e-5, 127.0, 1.0 / 127.0, 1.0 / NW, 1048576.0, 2.0 / 127.0]
CIDX = {v: i for i, v in enumerate(CONST_VALS)}

DEBUG_TENSORS = {
    "qrope": ([128, NCH * S], "bf"),
    "krope": ([128, NCH * S], "bf"),
    "vsb": ([128, NT * H * 65], "bf"),
    "aor": ([128, NCH * S], "bf"),
    "denall": ([16, S], "f32"),
    "Etap": ([128, 16 * 512], "bf"),
    "mtap": ([128, 4 * 512], "bf"),
    "itap": ([128, 128], "bf"),
    "avtap": ([65, 512], "f32"),
    "rtap": ([2, S], "f32"),
    "recip": ([16, S], "f32"),
    "denall2": ([16, S], "f32"),
    "denvec": ([16, S], "f32"),
    "mcp": ([128, 512], "f32"),
    "sctap": ([128, 512], "f32"),
    "dent": ([128, S], "f32"),
    "aon": ([128, NCH * S], "bf"),
    "y1n": ([S, D], "f32"),
    "sttap": ([128, 16], "f32"),
    "xq": ([D, S], "bf"),
    "terng": ([D, INNER], "bf"),
    "ternd": ([INNER, D], "bf"),
    "hq": ([128, NSUP * NTS * INNER], "bf"),
}


def build_program(stop_phase=99, debug=(), max_pairs=None, dbg_mode=None):
    nc = bass.Bass("TRN2", target_bir_lowering=False)

    d_xT = nc.dram_tensor("xT", [D, S], F32, kind="ExternalInput").ap()
    d_x = nc.dram_tensor("x", [S, D], F32, kind="ExternalInput").ap()
    d_wqkv = nc.dram_tensor("wqkv", [D, 3 * D], F32, kind="ExternalInput").ap()
    d_wout = nc.dram_tensor("wout", [D, D], F32, kind="ExternalInput").ap()
    d_wgT = nc.dram_tensor("wgT", [D, INNER], F32, kind="ExternalInput").ap()
    d_wuT = nc.dram_tensor("wuT", [D, INNER], F32, kind="ExternalInput").ap()
    d_wdT = nc.dram_tensor("wdT", [INNER, D], F32, kind="ExternalInput").ap()
    d_consts = nc.dram_tensor("consts", [128, len(CONST_VALS)], F32, kind="ExternalInput").ap()
    d_cos = nc.dram_tensor("cosf", [128, S], F32, kind="ExternalInput").ap()
    d_sin = nc.dram_tensor("sinf", [128, S], F32, kind="ExternalInput").ap()
    d_ident = nc.dram_tensor("ident", [128, 128], BF16, kind="ExternalInput").ap()
    d_masks = nc.dram_tensor("masks", [128, 4 * 512], BF16, kind="ExternalInput").ap()
    d_sel2 = nc.dram_tensor("sel2", [2, 128], F32, kind="ExternalInput").ap()
    d_ones1 = nc.dram_tensor("ones1", [1, 128], F32, kind="ExternalInput").ap()
    d_out = nc.dram_tensor("out", [S, D], F32, kind="ExternalOutput").ap()
    d_terng = nc.dram_tensor("terng", [D, INNER], BF16, kind="Internal").ap()
    d_ternu = nc.dram_tensor("ternu", [D, INNER], BF16, kind="Internal").ap()
    d_ternd = nc.dram_tensor("ternd", [INNER, D], BF16, kind="Internal").ap()
    d_y1s = nc.dram_tensor("y1s", [S, D], F32, kind="Internal").ap()

    d_xqs = nc.dram_tensor("xqs", [D, S], BF16, kind="Internal").ap()

    dbg = {}
    for name in debug:
        shp, dt = DEBUG_TENSORS[name]
        dbg[name] = nc.dram_tensor("dbg_" + name, shp,
                                   BF16 if dt == "bf" else F32,
                                   kind="ExternalOutput").ap()

    es = contextlib.ExitStack()
    with es:
        s_consts = es.enter_context(nc.sbuf_tensor("c_consts", [128, len(CONST_VALS)], F32, side="right"))
        s_ident = es.enter_context(nc.sbuf_tensor("c_ident", [128, 128], BF16, side="right"))
        s_amx = es.enter_context(nc.sbuf_tensor("p_amx", [128, NT], F32, side="right"))
        s_thr = es.enter_context(nc.sbuf_tensor("p_thr", [128, 8], F32, side="right"))

        def CC(v):
            return s_consts[:, CIDX[v]:CIDX[v] + 1]

        for i, v in enumerate(CONST_VALS):
            nc.const_aps.register(F32, v, s_consts[:, i:i + 1])

        class Ctr:
            def __init__(self, name):
                self.h = es.enter_context(nc.semaphore(name))
                self.n = 0

            def inc(self, instr, k=1):
                instr.then_inc(self.h, k)
                self.n += k
                return self.n

            def dinc(self, instr):
                return self.inc(instr, 16)

        dS = Ctr("dS")
        dC2 = Ctr("dC2")
        dG = Ctr("dG")
        dO = Ctr("dO")
        dAor = Ctr("dAor")
        dDen = [Ctr("dDen0"), Ctr("dDen1")]
        dRst = [Ctr("dRst0"), Ctr("dRst1")]
        dWo = Ctr("dWo")
        dX = [Ctr("dX0"), Ctr("dX1")]
        dY1n = [Ctr("dY1n0"), Ctr("dY1n1")]
        dXq = [Ctr("dXq0"), Ctr("dXq1")]
        dW5 = [Ctr("dW50"), Ctr("dW51"), Ctr("dW52")]
        tpA = Ctr("tpA")
        thrV = Ctr("thrV")
        thrT = Ctr("thrT")
        ts2 = Ctr("ts2")
        tv2 = Ctr("tv2")
        dTst = [Ctr(f"dTst{i}") for i in range(4)]
        dAcc = [Ctr(f"dAcc{i}") for i in range(4)]
        dXql = [Ctr("dXql0"), Ctr("dXql1")]
        dGu = [Ctr("dGu0"), Ctr("dGu1")]
        dTd = [Ctr("dTd0"), Ctr("dTd1")]
        dY1l = [Ctr("dY1l0"), Ctr("dY1l1")]
        dOut = Ctr("dOut")
        te = Ctr("te")
        ve = Ctr("ve")
        se = Ctr("se")
        ge = Ctr("ge")

        block = es.enter_context(nc.Block())

        def _scope():
            return es.enter_context(contextlib.ExitStack())

        scB = _scope()
        scA = _scope()
        psA = _scope()

        def mk(stack, name, shape, dt=F32, side=None):
            kw = {} if side is None else {"side": side}
            return stack.enter_context(nc.sbuf_tensor(name, shape, dt, **kw))

        s_q = mk(scB, "a_q", [128, NCH * S], BF16)
        s_k = mk(scB, "a_k", [128, NCH * S], BF16)
        s_v = mk(scB, "a_v", [128, NT * H * 65], BF16)
        s_cos = mk(scA, "a_cos", [128, S])
        s_sin = mk(scA, "a_sin", [128, S])
        s_xfm = mk(scA, "a_xfm", [128, NCH * S], BF16)
        s_wqk = mk(scA, "a_wqk", [128, NCH * 2048], BF16)
        s_wv = mk(scA, "a_wv", [128, NCH * 1024], BF16)
        s_tmpA = mk(scA, "a_tmpA", [128, S], BF16)
        s_tmpB = mk(scA, "a_tmpB", [128, S], BF16)
        s_swp = mk(scA, "a_swp", [128, S], BF16)

        p_qk = [psA.enter_context(nc.psum_tensor(f"p_qk{i}", [128, 512], F32))
                for i in range(4)]

        @block.sync
        def _(sy):
            for dst, src in [(s_consts[:], d_consts), (s_ident[:], d_ident),
                             (s_cos[:], d_cos), (s_sin[:], d_sin)]:
                dS.dinc(sy.dma_start(out=dst, in_=src))
        ph0_sync = dS.n

        @block.gpsimd
        def _(g):
            for c in range(NCH):
                dG.dinc(g.dma_start(out=s_xfm[:, c * S:(c + 1) * S],
                                    in_=d_xT[c * 128:(c + 1) * 128, :]))
            for c in range(NCH):
                dG.dinc(g.dma_start(out=s_wqk[:, c * 2048:(c + 1) * 2048],
                                    in_=d_wqkv[c * 128:(c + 1) * 128, 0:2048]))
                dG.dinc(g.dma_start(out=s_wv[:, c * 1024:(c + 1) * 1024],
                                    in_=d_wqkv[c * 128:(c + 1) * 128, 2048:3072]))
        ph0_gp = dG.n

        ve0, se0, te0 = ve.n, se.n, te.n

        @block.tensor
        def _(t):
            t.wait_ge(dS.h, ph0_sync)
            t.wait_ge(dG.h, ph0_gp)
            gi = 0
            for sl in range(16):
                for ntile in range(4):
                    pb = p_qk[gi % 4]
                    if gi >= 4:
                        t.wait_ge(se.h, se0 + gi - 3)
                    for kc in range(NCH):
                        mm = t.matmul(pb[:],
                                      s_wqk[:, kc * 2048 + sl * 128: kc * 2048 + (sl + 1) * 128],
                                      s_xfm[:, kc * S + ntile * 512: kc * S + (ntile + 1) * 512],
                                      start=(kc == 0), stop=(kc == NCH - 1))
                    te.inc(mm)
                    gi += 1
            for tt in range(NT):
                for vs in range(2):
                    pb = p_qk[gi % 4]
                    if gi - 4 < 64:
                        t.wait_ge(se.h, se0 + gi - 3)
                    else:
                        t.wait_ge(ve.h, ve0 + NT + (gi - 4 - 64) + 1)
                    for kc in range(NCH):
                        mm = t.matmul(pb[:],
                                      s_xfm[:, kc * S + tt * 128: kc * S + (tt + 1) * 128],
                                      s_wv[:, kc * 1024 + vs * 512: kc * 1024 + (vs + 1) * 512],
                                      start=(kc == 0), stop=(kc == NCH - 1))
                    te.inc(mm)
                    gi += 1
        te_p1 = te.n

        @block.scalar
        def _(s):
            for gi in range(64):
                s.wait_ge(te.h, te0 + gi + 1)
                sl, ntile = divmod(gi, 4)
                dst = s_q if sl < 8 else s_k
                c = sl % 8
                se.inc(s.activation(dst[:, c * S + ntile * 512: c * S + (ntile + 1) * 512],
                                    p_qk[gi % 4][:], AF.Copy))
        se_p1 = se.n

        @block.vector
        def _(v):
            for t_ in range(NT):
                ap = s_v[:, t_ * H * 65:(t_ + 1) * H * 65].rearrange(
                    "p (h w) -> p h w", h=H)[:, :, 64:65]
                v.memset(ap, 1.0)
            ve.inc(v.nop(), NT)
            for gi in range(64, 96):
                v.wait_ge(te.h, te0 + gi + 1)
                tt, vs = divmod(gi - 64, 2)
                base = tt * H * 65
                pb = p_qk[gi % 4]
                dst = s_v[:, base + (vs * 8) * 65: base + (vs * 8 + 8) * 65].rearrange(
                    "p (h w) -> p h w", h=8)[:, :, 0:64]
                src = pb[:].rearrange("p (h w) -> p h w", h=8)
                ve.inc(v.tensor_copy(dst, src))
        ve_vevac = ve.n

        @block.gpsimd
        def _(g):
            for ci in range(16):
                qk, c = divmod(ci, 8)
                src = s_q if qk == 0 else s_k
                g.wait_ge(se.h, se0 + (qk * 8 + c) * 4 + 4)
                if ci >= 1:
                    g.wait_ge(ve.h, ve_vevac + (ci - 1) * 2 + 1)
                for hh in range(2):
                    o = hh * 64
                    dO.dinc(g.dma_start(out=s_swp[o + 32: o + 64, :],
                                        in_=src[o: o + 32, c * S:(c + 1) * S]))
                    dO.dinc(g.dma_start(out=s_swp[o: o + 32, :],
                                        in_=src[o + 32: o + 64, c * S:(c + 1) * S]))
        dO_p1 = dO.n

        @block.vector
        def _(v):
            for ci in range(16):
                qk, c = divmod(ci, 8)
                src = s_q if qk == 0 else s_k
                sl = src[:, c * S:(c + 1) * S]
                v.wait_ge(dO.h, (ci + 1) * 64)
                v.drain()
                v.tensor_tensor(s_tmpA[:], sl, s_cos[:], ALU.mult)
                ve.inc(v.tensor_tensor(s_tmpB[:], s_swp[:], s_sin[:], ALU.mult))
                v.drain()
                ve.inc(v.tensor_tensor(sl, s_tmpA[:], s_tmpB[:], ALU.add))
        ve_rope = ve.n

        if "qrope" in dbg:
            @block.sync
            def _(sy):
                sy.wait_ge(ve.h, ve_rope)
                dS.dinc(sy.dma_start(out=dbg["qrope"], in_=s_q[:]))
                dS.dinc(sy.dma_start(out=dbg["krope"], in_=s_k[:]))
        if "vsb" in dbg:
            @block.sync
            def _(sy):
                sy.wait_ge(ve.h, ve_vevac)
                dS.dinc(sy.dma_start(out=dbg["vsb"], in_=s_v[:]))

        ctrs = ([dS, dC2, dG, dO, dAor, dWo, dTst[0], dTst[1], dTst[2], dTst[3], dAcc[0], dAcc[1], dAcc[2], dAcc[3], dXql[0], dXql[1], dOut, te, ve, se, ge, tpA, thrV, thrT, ts2, tv2]
                + dDen + dRst + dX + dY1n + dXq + dW5 + dGu + dTd + dY1l)
        if stop_phase <= 1:
            _finish(block, ctrs)
            return nc

        scA.close()
        psA.close()
        scD = _scope()
        scC = _scope()
        psB = _scope()
        s_E = mk(scC, "a_E", [128, 16 * 512], BF16)
        s_masks = mk(scC, "a_masks", [128, 4 * 512], BF16)

        @block.sync
        def _(sy):
            sy.wait_ge(te.h, te_p1)
            sy.wait_ge(ve.h, ve_rope)
            sy.wait_ge(dO.h, dO_p1)
            dC2.dinc(sy.dma_start(out=s_masks[:], in_=d_masks))
        dS_masks = dC2.n
        s_dent = [mk(scC, f"a_dent{i}", [128, S]) for i in range(2)]
        s_odstb = mk(scC, "a_odstb", [128, 512], BF16)
        s_aor = mk(scD, "d_aor", [128, NCH * S], BF16, side="right")
        s_denall = mk(scD, "d_denall", [16, S], side="right")
        s_wst = [mk(scD, f"t_wst{i}", [128, 1024], side="right")
                 for i in range(3)]
        s_msum = mk(scD, "t_msum", [128, 96], side="right")
        s_red = mk(scD, "t_red", [128, 3], side="right")
        s_t11 = mk(scD, "t_t11", [1, 1], side="right")
        s_ones1 = mk(scD, "t_ones1", [1, 128], side="right")
        s_ta = [mk(scD, f"t_ta{i}", [128, 1024], BF16, side="right")
                for i in range(4)]
        s_tb = [mk(scD, f"t_tb{i}", [128, 1024], BF16, side="right")
                for i in range(4)]
        WCHUNKS = []
        for mi6, (dsrc6, ddst6, ntl6, wid6) in enumerate(
                ((d_wgT, d_terng, 8, 4096), (d_wuT, d_ternu, 8, 4096),
                 (d_wdT, d_ternd, 32, 1024))):
            for ti6 in range(ntl6):
                for cc6 in range(wid6 // 1024):
                    WCHUNKS.append((mi6,
                                    dsrc6[ti6 * 128:(ti6 + 1) * 128,
                                          cc6 * 1024:(cc6 + 1) * 1024],
                                    ddst6[ti6 * 128:(ti6 + 1) * 128,
                                          cc6 * 1024:(cc6 + 1) * 1024]))
        NWCH = len(WCHUNKS)

        def pvW(L):
            return ((L // 3) + 1) * 16

        @block.sync
        def _(sy):
            dS.dinc(sy.dma_start(out=s_ones1[:], in_=d_ones1))
        dS_ones1 = dS.n
        p_sc = [psB.enter_context(nc.psum_tensor(f"p_sc{i}", [128, 512], F32))
                for i in range(4)]
        p_av = [psB.enter_context(nc.psum_tensor(f"p_av{i}", [65, 512], F32))
                for i in range(2)]

        pairs = [(h, j) for h in range(H) for j in range(NJ)]
        if max_pairs is not None:
            pairs = pairs[:max_pairs]
        nkc = {p: 4 * p[1] + 4 for p in pairs}
        se_base = {}
        run = se.n
        for p in pairs:
            se_base[p] = run
            run += nkc[p]
        se_p2_end = run
        te_sc_base = {}
        te_av_done = {}
        trun = te.n
        for p in pairs:
            te_sc_base[p] = trun
            trun += nkc[p] + 1
            te_av_done[p] = trun
        ve_evac = {}
        vrun = ve.n
        for i, p in enumerate(pairs):
            ve_evac[p] = vrun + i + 1
        ve_p2_end = vrun + len(pairs)
        dO_evt = {}
        arun = dAor.n
        drun = [dDen[0].n, dDen[1].n]
        for h in range(H):
            for j in range(NJ):
                if (h, j) not in nkc:
                    continue
                if h % 2 == 1:
                    arun += 16
                    dO_evt[("aor", h, j)] = arun
            if (h, NJ - 1) in nkc:
                drun[h % 2] += 16
                dO_evt[("den", h)] = (h % 2, drun[h % 2])


        @block.tensor
        def _(t):
            t.wait_ge(ve.h, ve_rope)
            t.wait_ge(dC2.h, 32)
            for pi, (h, j) in enumerate(pairs):
                ch, hp = divmod(h, 2)
                for kc in range(nkc[(h, j)]):
                    pb = p_sc[kc % 4]
                    if kc >= 4:
                        t.wait_ge(se.h, se_base[(h, j)] + kc - 3)
                    elif pi > 0:
                        prev = pairs[pi - 1]
                        kcp = max(kk for kk in range(nkc[prev]) if kk % 4 == kc % 4)
                        t.wait_ge(se.h, se_base[prev] + kcp + 1)
                    r = kc - 4 * j
                    if dbg_mode == "nomask":
                        mm = t.matmul(pb[:],
                                      s_k[hp * 64: hp * 64 + 64,
                                          ch * S + kc * 128: ch * S + (kc + 1) * 128],
                                      s_q[hp * 64: hp * 64 + 64,
                                          ch * S + j * 512: ch * S + (j + 1) * 512],
                                      start=True, stop=True)
                    elif dbg_mode == "noscores":
                        mm = t.matmul(pb[:], s_ident[:],
                                      s_masks[:, max(r, 0) * 512:(max(r, 0) + 1) * 512],
                                      start=True, stop=True)
                    else:
                        mm = t.matmul(pb[:],
                                      s_k[hp * 64: hp * 64 + 64,
                                          ch * S + kc * 128: ch * S + (kc + 1) * 128],
                                      s_q[hp * 64: hp * 64 + 64,
                                          ch * S + j * 512: ch * S + (j + 1) * 512],
                                      start=True, stop=(r < 0))
                        if r >= 0:
                            mm = t.matmul(pb[:], s_ident[:],
                                          s_masks[:, r * 512:(r + 1) * 512],
                                          start=False, stop=True)
                    te.inc(mm)
                pv = p_av[pi % 2]
                if pi >= 2:
                    t.wait_ge(ve.h, ve_evac[pairs[pi - 2]])
                for kc in range(nkc[(h, j)]):
                    t.wait_ge(se.h, se_base[(h, j)] + kc + 1)
                    vbase = kc * H * 65 + h * 65
                    mm = t.matmul(pv[:], s_v[:, vbase: vbase + 65],
                                  s_E[:, kc * 512:(kc + 1) * 512],
                                  start=(kc == 0), stop=(kc == nkc[(h, j)] - 1))
                te.inc(mm)
        assert te.n == trun, (te.n, trun)

        @block.scalar
        def _(s):
            for (h, j) in pairs:
                for kc in range(nkc[(h, j)]):
                    s.wait_ge(te.h, te_sc_base[(h, j)] + kc + 1)
                    se.inc(s.activation(s_E[:, kc * 512:(kc + 1) * 512],
                                        p_sc[kc % 4][:], AF.Exp,
                                        bias=-CSTAB, scale=0.125))
        assert se.n == se_p2_end

        tern_emitA = [0]

        @block.vector
        def _(v):
            def _passA(v, hi):
                while tern_emitA[0] < hi:
                    i6 = tern_emitA[0]
                    v.wait_ge(dW5[i6 % 3].h, pvW(i6))
                    tpA.inc(v.tensor_reduce(s_msum[:, i6:i6 + 1],
                                            s_wst[i6 % 3][:], AXX, ALU.add,
                                            apply_absolute_value=True))
                    tern_emitA[0] += 1
            for pi, (h, j) in enumerate(pairs):
                ch, hp = divmod(h, 2)
                pv = p_av[pi % 2]
                v.wait_ge(te.h, te_av_done[(h, j)])
                if h % 2 == 0:
                    v.tensor_copy(s_aor[0:64, ch * S + j * 512: ch * S + (j + 1) * 512],
                                  pv[0:64, :])
                else:
                    if j > 0:
                        v.wait_ge(dAor.h, dO_evt[("aor", h, j - 1)])
                    elif h >= 3:
                        v.wait_ge(dAor.h, dO_evt[("aor", h - 2, NJ - 1)])
                    v.tensor_copy(s_odstb[0:64, :], pv[0:64, :])
                if h >= 2:
                    par, val = dO_evt[("den", h - 2)]
                    v.wait_ge(dDen[par].h, val)
                ve.inc(v.tensor_copy(s_dent[h % 2][64:65, j * 512:(j + 1) * 512],
                                     pv[64:65, :]))
                _passA(v, ((pi + 1) * NWCH) // len(pairs))
            _passA(v, NWCH)
        assert ve.n == ve_p2_end
        assert tpA.n == NWCH

        tern_emitL = [0]

        @block.gpsimd
        def _(g):
            def _loadA(g, hi):
                while tern_emitL[0] < hi:
                    L = tern_emitL[0]
                    if L >= 3:
                        g.wait_ge(tpA.h, L - 2)
                    dW5[L % 3].dinc(g.dma_start(out=s_wst[L % 3][:],
                                                in_=WCHUNKS[L][1]))
                    tern_emitL[0] += 1
            slot = 0
            for h in range(H):
                ch = h // 2
                for j in range(NJ):
                    if (h, j) not in ve_evac:
                        continue
                    if h % 2 == 1:
                        g.wait_ge(ve.h, ve_evac[(h, j)])
                        dAor.dinc(g.dma_start(
                            out=s_aor[64:128, ch * S + j * 512: ch * S + (j + 1) * 512],
                            in_=s_odstb[0:64, :]))
                    slot += 1
                    _loadA(g, (slot * NWCH) // len(pairs))
                if (h, NJ - 1) in ve_evac:
                    g.wait_ge(ve.h, ve_evac[(h, NJ - 1)])
                    dDen[h % 2].dinc(g.dma_start(out=s_denall[h:h + 1, :],
                                                 in_=s_dent[h % 2][64:65, :]))
            _loadA(g, NWCH)
        dO_p2_end = (dAor.n, dDen[0].n, dDen[1].n)

        dS_aortap = None
        if "aor" in dbg:
            @block.sync
            def _(sy):
                sy.wait_ge(ve.h, ve_p2_end)
                sy.wait_ge(dAor.h, dO_p2_end[0])
                dS.dinc(sy.dma_start(out=dbg["aor"], in_=s_aor[:]))
            dS_aortap = dS.n
        if "denall" in dbg:
            @block.sync
            def _(sy):
                sy.wait_ge(dDen[0].h, dO_p2_end[1])
                sy.wait_ge(dDen[1].h, dO_p2_end[2])
                dS.dinc(sy.dma_start(out=dbg["denall"], in_=s_denall[:]))
        if "sctap" in dbg:
            s_sctap = mk(scC, "a_sctap", [128, 512])

            @block.vector
            def _(v):
                v.wait_ge(se.h, se_p2_end)
                ve.inc(v.tensor_copy(s_sctap[:], p_sc[3][:]))
            _svc = ve.n

            @block.sync
            def _(sy):
                sy.wait_ge(ve.h, _svc)
                dS.dinc(sy.dma_start(out=dbg["sctap"], in_=s_sctap[:]))
        if "mcp" in dbg:
            s_mcp = mk(scC, "a_mcp", [128, 512])
            p_dbgm = psB.enter_context(nc.psum_tensor("p_dbgm", [128, 512], F32))

            @block.tensor
            def _(t):
                t.wait_ge(dC2.h, dS_masks)
                te.inc(t.matmul(p_dbgm[:], s_ident[:], s_masks[:, 3 * 512:4 * 512],
                                start=True, stop=True))
            _tmc = te.n

            @block.vector
            def _(v):
                v.wait_ge(te.h, _tmc)
                ve.inc(v.tensor_copy(s_mcp[:], p_dbgm[:]))
            _vmc = ve.n

            @block.sync
            def _(sy):
                sy.wait_ge(ve.h, _vmc)
                dS.dinc(sy.dma_start(out=dbg["mcp"], in_=s_mcp[:]))
        if "avtap" in dbg:
            s_avt = mk(scC, "a_avt", [65, 512])

            @block.vector
            def _(v):
                v.wait_ge(te.h, te_av_done[pairs[0]])
                ve.inc(v.tensor_copy(s_avt[:], p_av[0][:]))
            _vav = ve.n

            @block.sync
            def _(sy):
                sy.wait_ge(ve.h, _vav)
                dS.dinc(sy.dma_start(out=dbg["avtap"], in_=s_avt[:]))
        if "itap" in dbg:
            @block.sync
            def _(sy):
                sy.wait_ge(dS.h, ph0_sync)
                dS.dinc(sy.dma_start(out=dbg["itap"], in_=s_ident[:]))
        if "mtap" in dbg:
            @block.sync
            def _(sy):
                sy.wait_ge(dC2.h, dS_masks)
                dS.dinc(sy.dma_start(out=dbg["mtap"], in_=s_masks[:]))
        if "Etap" in dbg:
            @block.sync
            def _(sy):
                sy.wait_ge(ve.h, ve_p2_end)
                dS.dinc(sy.dma_start(out=dbg["Etap"], in_=s_E[:]))
        if "dent" in dbg:
            @block.sync
            def _(sy):
                sy.wait_ge(ve.h, ve_p2_end)
                dS.dinc(sy.dma_start(out=dbg["dent"], in_=s_dent[0][:]))

        if stop_phase <= 2:
            _finish(block, ctrs)
            return nc

        scC.close()
        scE = _scope()
        s_recip = mk(scE, "e_recip", [16, S])
        s_R = [mk(scE, f"e_R{i}", [128, S]) for i in range(2)]

        s_dencp = mk(scE, "e_dencp", [16, S]) if "denvec" in dbg else None

        @block.vector
        def _(v):
            v.wait_ge(dAor.h, dO_p2_end[0])
            v.wait_ge(dDen[0].h, dO_p2_end[1])
            v.wait_ge(dDen[1].h, dO_p2_end[2])
            if s_dencp is not None:
                v.tensor_copy(s_dencp[:], s_denall[:])
            c20 = s_consts[0:16, CIDX[1048576.0]:CIDX[1048576.0] + 1]
            c0_16 = s_consts[0:16, CIDX[0.0]:CIDX[0.0] + 1]
            v.tensor_scalar(s_recip[:], s_denall[:], c20, c0_16, ALU.mult, ALU.add)
            v.drain()
            v.reciprocal(s_recip[:], s_recip[:])
            v.drain()
            ve.inc(v.tensor_scalar(s_recip[:], s_recip[:], c20, c0_16,
                                   ALU.mult, ALU.add))
        ve_recip = ve.n

        psB.close()
        s_sel2 = mk(scE, "e_sel2", [2, 128])
        s_rst = [mk(scE, f"e_rst{i}", [2, S]) for i in range(2)]
        p_R = [scE.enter_context(nc.psum_tensor(f"p_R{i}", [128, 512], F32))
               for i in range(2)]

        @block.sync
        def _(sy):
            dC2.dinc(sy.dma_start(out=s_sel2[:], in_=d_sel2))
        dS_sel2 = dC2.n

        te_R = {}
        trunR = te.n
        for c in range(NCH):
            for qq in range(4):
                trunR += 1
                te_R[(c, qq)] = trunR
        ve_normq = {}
        vrun = ve.n
        for c in range(NCH):
            for qq in range(4):
                vrun += 1
                ve_normq[(c, qq)] = vrun
        dO_rst = {}
        _rr = [dRst[0].n, dRst[1].n]
        for c in range(NCH):
            _rr[c % 2] += 16
            dO_rst[c] = (c % 2, _rr[c % 2])

        @block.gpsimd
        def _(g):
            for c in range(NCH):
                g.wait_ge(ve.h, ve_recip)
                if c >= 2:
                    g.wait_ge(te.h, te_R[(c - 2, 3)])
                dRst[c % 2].dinc(g.dma_start(out=s_rst[c % 2][:],
                                             in_=s_recip[2 * c:2 * c + 2, :]))

        @block.tensor
        def _(t):
            t.wait_ge(dC2.h, 32)
            for c in range(NCH):
                t.wait_ge(dRst[c % 2].h, dO_rst[c][1])
                for qq in range(4):
                    if (c, qq) != (0, 0):
                        pass
                    pb = p_R[(c * 4 + qq) % 2]
                    idx = c * 4 + qq - 2
                    if idx >= 0:
                        t.wait_ge(ve.h, ve_normq[(idx // 4, idx % 4)])
                    te.inc(t.matmul(pb[:], s_sel2[:],
                                    s_rst[c % 2][:, qq * 512:(qq + 1) * 512],
                                    start=True, stop=True))

        @block.vector
        def _(v):
            if dS_aortap is not None:
                v.wait_ge(dS.h, dS_aortap)
            for c in range(NCH):
                for qq in range(4):
                    v.wait_ge(te.h, te_R[(c, qq)])
                    pb = p_R[(c * 4 + qq) % 2]
                    sl = s_aor[:, c * S + qq * 512: c * S + (qq + 1) * 512]
                    ve.inc(v.tensor_tensor(sl, sl, pb[:], ALU.mult))
        ve_p3 = ve.n

        if "denall2" in dbg:
            @block.sync
            def _(sy):
                sy.wait_ge(ve.h, ve_recip)
                dS.dinc(sy.dma_start(out=dbg["denall2"], in_=s_denall[:]))
        if "denvec" in dbg:
            @block.sync
            def _(sy):
                sy.wait_ge(ve.h, ve_recip)
                dS.dinc(sy.dma_start(out=dbg["denvec"], in_=s_dencp[:]))
        if "rtap" in dbg:
            @block.sync
            def _(sy):
                sy.wait_ge(ve.h, ve_p3)
                dS.dinc(sy.dma_start(out=dbg["rtap"], in_=s_rst[0][:]))
                dS.dinc(sy.dma_start(out=dbg["recip"], in_=s_recip[:]))
        if "aon" in dbg:
            @block.sync
            def _(sy):
                sy.wait_ge(ve.h, ve_p3)
                dS.dinc(sy.dma_start(out=dbg["aon"], in_=s_aor[:]))

        if stop_phase <= 3:
            _finish(block, ctrs)
            return nc

        scE.close()
        scB.close()
        scF = _scope()
        psC = _scope()
        s_wout = mk(scF, "f_wout", [128, NCH * D], BF16)
        s_xt = [mk(scF, f"f_xt{i}", [128, D]) for i in range(2)]
        s_y1 = mk(scF, "f_y1", [128, D])
        s_y1n = [mk(scF, f"f_y1n{i}", [128, D]) for i in range(2)]
        s_z = mk(scF, "f_z", [128, D])
        s_tr2 = mk(scF, "f_tr2", [128, D], BF16)
        s_i32 = mk(scF, "f_i32", [128, D], I32)
        s_r1 = mk(scF, "f_r1", [128, D])
        s_u = mk(scF, "f_u", [128, D], BF16)
        s_xqtm = mk(scF, "f_xqtm", [128, D], BF16)
        s_xqst = [mk(scF, f"f_xqst{i}", [128, D], BF16) for i in range(2)]
        s_st = mk(scF, "f_st", [128, 16])

        p_o = [psC.enter_context(nc.psum_tensor(f"p_o{i}", [128, 512], F32))
               for i in range(4)]
        p_tr = [psC.enter_context(nc.psum_tensor(f"p_tr{i}", [128, 512], BF16))
                for i in range(2)]
        p_m1 = psC.enter_context(nc.psum_tensor("p_m1b", [1, 1], F32))
        p_m2 = psC.enter_context(nc.psum_tensor("p_m2b", [128, 1], F32))

        ST = {k: i for i, k in enumerate(
            ["sum", "sumsq", "mean", "msq", "var", "rec", "rstd", "negmr",
             "amax", "rec2", "s1"])}

        def stc(name):
            return s_st[:, ST[name]:ST[name] + 1]

        VE_PER = 22
        SE_PER = 3
        ve4_0 = ve.n
        se4_0 = se.n
        te4_0 = te.n
        te_o_end = te4_0 + 2 * NT

        @block.gpsimd
        def _(g):
            g.wait_ge(te.h, trunR)
            for c in range(NCH):
                dWo.dinc(g.dma_start(out=s_wout[:, c * D:(c + 1) * D],
                                    in_=d_wout[c * 128:(c + 1) * 128, :]))
            for i6 in range(NWCH):
                L = NWCH + i6
                if i6 < 3:
                    g.wait_ge(tpA.h, NWCH)
                else:
                    g.wait_ge(ts2.h, 2 * (i6 - 3) + 2)
                dW5[L % 3].dinc(g.dma_start(out=s_wst[L % 3][:],
                                            in_=WCHUNKS[i6][1]))
                if i6 >= 1:
                    p6 = i6 - 1
                    g.wait_ge(ts2.h, 2 * p6 + 1)
                    dAcc[p6 % 4].dinc(g.dma_start(
                        out=WCHUNKS[p6][2], in_=s_ta[p6 % 4][:]))
                if i6 >= 3:
                    p6 = i6 - 3
                    g.wait_ge(ts2.h, 2 * p6 + 2)
                    g.wait_ge(dAcc[p6 % 4].h, ((p6 // 4) + 1) * 16)
                    dTst[p6 % 4].dinc(g.dma_start(
                        out=WCHUNKS[p6][2], in_=s_tb[p6 % 4][:],
                        accum_op=ALU.add))
            for p6 in (NWCH - 1,):
                g.wait_ge(ts2.h, 2 * p6 + 1)
                dAcc[p6 % 4].dinc(g.dma_start(
                    out=WCHUNKS[p6][2], in_=s_ta[p6 % 4][:]))
            for p6 in (NWCH - 3, NWCH - 2, NWCH - 1):
                g.wait_ge(ts2.h, 2 * p6 + 2)
                g.wait_ge(dAcc[p6 % 4].h, ((p6 // 4) + 1) * 16)
                dTst[p6 % 4].dinc(g.dma_start(
                    out=WCHUNKS[p6][2], in_=s_tb[p6 % 4][:],
                    accum_op=ALU.add))
        dG_wout = dWo.n

        dS_xland = {}
        dS_y1n = {}
        dS_xq = {}

        @block.sync
        def _(sy):
            sy.wait_ge(te.h, trunR)
            sy.wait_ge(ve.h, ve_p3)
            for nt in range(2):
                dX[nt % 2].dinc(sy.dma_start(out=s_xt[nt % 2][:],
                                             in_=d_x[nt * 128:(nt + 1) * 128, :]))
                dS_xland[nt] = (nt % 2, dX[nt % 2].n)
            for nt in range(NT):
                sy.wait_ge(se.h, se4_0 + nt * SE_PER + 3)
                dY1n[nt % 2].dinc(sy.dma_start(out=d_y1s[nt * 128:(nt + 1) * 128, :],
                                               in_=s_y1n[nt % 2][:]))
                dS_y1n[nt] = (nt % 2, dY1n[nt % 2].n)
                sy.wait_ge(ve.h, ve4_0 + nt * VE_PER + 22)
                st = s_xqst[nt % 2]
                outap = bass.AP(d_xqs.tensor, nt * 128,
                                [[S, 128], [S * 128, NCH], [1, 128]])
                inap = st[:].rearrange("p (c w) -> p c w", c=NCH)
                dXq[nt % 2].dinc(sy.dma_start(out=outap, in_=inap))
                dS_xq[nt] = (nt % 2, dXq[nt % 2].n)
                if nt + 2 < NT:
                    sy.wait_ge(ve.h, ve4_0 + nt * VE_PER + 2)
                    dX[nt % 2].dinc(sy.dma_start(out=s_xt[nt % 2][:],
                                                 in_=d_x[(nt + 2) * 128:(nt + 3) * 128, :]))
                    dS_xland[nt + 2] = (nt % 2, dX[nt % 2].n)
        dS_p4 = (dY1n[0].n, dY1n[1].n, dXq[0].n, dXq[1].n)

        te_evt = {}
        trun4 = te.n
        sched = []
        for nt in range(NT):
            sched.append(("o", nt))
            if nt >= 2:
                sched.append(("t", nt - 2))
        sched.append(("t", NT - 2))
        sched.append(("t", NT - 1))
        for kind, nt in sched:
            if kind == "o":
                for half in range(2):
                    trun4 += 1
                    te_evt[("o", nt, half)] = trun4
            else:
                for b in range(2):
                    trun4 += 1
                    te_evt[("t", nt, b)] = trun4

        @block.tensor
        def _(t):
            t.wait_ge(dWo.h, dG_wout)
            t.wait_ge(ve.h, ve_p3)
            t.wait_ge(dS.h, dS_ones1)
            for mi6 in range(3):
                t.wait_ge(thrV.h, 3 if mi6 == 0 else 2 * mi6 + 2)
                thrT.inc(t.matmul(p_m1[:], s_red[:, mi6:mi6 + 1], CC(1.0),
                                  start=True, stop=True))
                t.wait_ge(thrV.h, 2 * mi6 + 4)
                thrT.inc(t.matmul(p_m2[:], s_ones1[:], s_t11[:],
                                  start=True, stop=True))
            for kind, nt in sched:
                if kind == "o":
                    for half in range(2):
                        pb = p_o[(nt * 2 + half) % 4]
                        grp = nt * 2 + half - 4
                        if grp >= 0:
                            t.wait_ge(ve.h, ve4_0 + (grp // 2) * VE_PER + (grp % 2) + 1)
                        for kc in range(NCH):
                            mm = t.matmul(pb[:],
                                          s_aor[:, kc * S + nt * 128:kc * S + (nt + 1) * 128],
                                          s_wout[:, kc * D + half * 512: kc * D + (half + 1) * 512],
                                          start=(kc == 0), stop=(kc == NCH - 1))
                        te.inc(mm)
                else:
                    t.wait_ge(ve.h, ve4_0 + nt * VE_PER + 20)
                    for b in range(2):
                        pb = p_tr[b]
                        if nt > 0:
                            t.wait_ge(ve.h, ve4_0 + (nt - 1) * VE_PER + 20 + b + 1)
                        for kk in range(4):
                            kc = b * 4 + kk
                            mm = t.transpose(pb[:, kk * 128:(kk + 1) * 128],
                                             s_xqtm[:, kc * 128:(kc + 1) * 128],
                                             s_ident[:])
                        te.inc(mm)
        te_tr_end = te.n
        assert te_tr_end == trun4 == te4_0 + 4 * NT

        @block.vector
        def _(v):
            v.wait_ge(tpA.h, NWCH)
            for mi6 in range(3):
                v.drain()
                thrV.inc(v.tensor_reduce(s_red[:, mi6:mi6 + 1],
                                         s_msum[:, mi6 * 32:(mi6 + 1) * 32],
                                         AXX, ALU.add))
            for mi6 in range(3):
                v.wait_ge(thrT.h, 2 * mi6 + 1)
                v.drain()
                thrV.inc(v.tensor_copy(s_t11[:], p_m1[:]))
                v.wait_ge(thrT.h, 2 * mi6 + 2)
                thrV.inc(v.tensor_copy(s_thr[:, mi6:mi6 + 1], p_m2[:]))
            v.drain()
            thrV.inc(v.tensor_scalar(s_thr[:, 0:3], s_thr[:, 0:3],
                                     CC(1.0 / NW), CC(1e-5), ALU.mult, ALU.max))
            v.drain()
            thrV.inc(v.tensor_scalar(s_thr[:, 0:3], s_thr[:, 0:3],
                                     CC(0.5), CC(1.0), ALU.mult, ALU.mult))
            v.drain()
            thrV.inc(v.tensor_scalar(s_thr[:, 4:7], s_thr[:, 0:3],
                                     CC(-1.0), CC(0.0), ALU.mult, ALU.add))
            v.drain()

            for nt in range(NT):
                v.wait_ge(te.h, te_evt[("o", nt, 1)])
                v.wait_ge(dX[dS_xland[nt][0]].h, dS_xland[nt][1])
                for half in range(2):
                    pb = p_o[(nt * 2 + half) % 4]
                    ve.inc(v.tensor_tensor(s_y1[:, half * 512:(half + 1) * 512],
                                           s_xt[nt % 2][:, half * 512:(half + 1) * 512],
                                           pb[:], ALU.add))
                v.drain()
                ve.inc(v.tensor_scalar(s_r1[:], s_y1[:], CC(1.0), CC(0.0),
                                       ALU.mult, ALU.add, accum_out=stc("sum")))
                v.drain()
                v.wait_ge(se.h, se4_0 + nt * SE_PER + 1)
                ve.inc(v.tensor_scalar(stc("mean"), stc("sum"), CC(1.0 / D), CC(0.0),
                                       ALU.mult, ALU.add))
                v.drain()
                ve.inc(v.tensor_tensor(stc("msq"), stc("mean"), stc("mean"), ALU.mult))
                v.drain()
                ve.inc(v.tensor_scalar(stc("var"), stc("sumsq"), CC(1.0 / D), CC(0.0),
                                       ALU.mult, ALU.add))
                v.drain()
                ve.inc(v.tensor_tensor(stc("var"), stc("var"), stc("msq"), ALU.subtract))
                v.drain()
                ve.inc(v.tensor_scalar(stc("var"), stc("var"), CC(1e-5), CC(1.0),
                                       ALU.add, ALU.mult))
                v.drain()
                ve.inc(v.reciprocal(stc("rec"), stc("var")))
                v.wait_ge(se.h, se4_0 + nt * SE_PER + 2)
                ve.inc(v.tensor_tensor(stc("negmr"), stc("mean"), stc("rstd"), ALU.mult))
                v.drain()
                ve.inc(v.tensor_scalar(stc("negmr"), stc("negmr"), CC(-1.0), CC(0.0),
                                       ALU.mult, ALU.add))
                v.wait_ge(se.h, se4_0 + nt * SE_PER + 3)
                ve.inc(v.tensor_reduce(stc("amax"), s_y1n[nt % 2][:], AXX, ALU.max,
                                       apply_absolute_value=True))
                v.drain()
                ve.inc(v.tensor_scalar(s_amx[:, nt:nt + 1], stc("amax"), CC(1e-5), CC(1.0),
                                       ALU.max, ALU.mult))
                v.drain()
                ve.inc(v.reciprocal(stc("rec2"), s_amx[:, nt:nt + 1]))
                v.drain()
                ve.inc(v.tensor_scalar(stc("s1"), stc("rec2"), CC(127.0), CC(0.0),
                                       ALU.mult, ALU.add))
                v.drain()
                ve.inc(v.tensor_scalar(s_z[:], s_y1n[nt % 2][:], stc("s1"), CC(512.0),
                                       ALU.mult, ALU.add))
                v.drain()
                ve.inc(v.tensor_copy(s_i32[:], s_z[:]))
                v.drain()
                ve.inc(v.tensor_copy(s_r1[:], s_i32[:]))
                v.drain()
                ve.inc(v.scalar_tensor_tensor(s_u[:], s_z[:], -0.5, s_r1[:],
                                              ALU.add, ALU.is_ge))
                v.drain()
                ve.inc(v.scalar_tensor_tensor(s_xqtm[:], s_r1[:], -512.0, s_u[:],
                                              ALU.add, ALU.add))
                for b in range(2):
                    v.wait_ge(te.h, te_evt[("t", nt, b)])
                    if nt >= 2:
                        v.wait_ge(dXq[dS_xq[nt - 2][0]].h, dS_xq[nt - 2][1])
                    ve.inc(v.tensor_copy(s_xqst[nt % 2][:, b * 512:(b + 1) * 512],
                                         p_tr[b][:]))
        ve_p4 = ve.n
        assert ve_p4 == ve4_0 + NT * VE_PER, (ve_p4, ve4_0)

        tern_emitS = [0]

        @block.scalar
        def _(s):
            def _signs(s, hi):
                while tern_emitS[0] < hi:
                    i6 = tern_emitS[0]
                    if i6 == 0:
                        s.wait_ge(thrV.h, 12)
                    s.wait_ge(dW5[(NWCH + i6) % 3].h, pvW(NWCH + i6))
                    if i6 >= 4:
                        s.wait_ge(dAcc[i6 % 4].h, (((i6 - 4) // 4) + 1) * 16)
                        s.wait_ge(dTst[i6 % 4].h, (((i6 - 4) // 4) + 1) * 16)
                    mi6 = WCHUNKS[i6][0]
                    ts2.inc(s.activation(s_ta[i6 % 4][:], s_wst[(NWCH + i6) % 3][:],
                                         AF.Sign,
                                         bias=s_thr[:, 4 + mi6:5 + mi6],
                                         scale=1.0))
                    ts2.inc(s.activation(s_tb[i6 % 4][:], s_wst[(NWCH + i6) % 3][:],
                                         AF.Sign,
                                         bias=s_thr[:, mi6:mi6 + 1],
                                         scale=1.0))
                    tern_emitS[0] += 1
            for nt in range(NT):
                s.wait_ge(ve.h, ve4_0 + nt * VE_PER + 2)
                se.inc(s.activation(s_tr2[:], s_y1[:], AF.Square, accum_out=stc("sumsq")))
                s.wait_ge(ve.h, ve4_0 + nt * VE_PER + 9)
                se.inc(s.activation(stc("rstd"), stc("rec"), AF.Sqrt))
                s.drain()
                s.wait_ge(ve.h, ve4_0 + nt * VE_PER + 11)
                if nt >= 2:
                    s.wait_ge(dY1n[dS_y1n[nt - 2][0]].h, dS_y1n[nt - 2][1])
                se.inc(s.activation(s_y1n[nt % 2][:], s_y1[:], AF.Identity,
                                    bias=stc("negmr"), scale=stc("rstd")))
                _signs(s, ((nt + 1) * NWCH) // NT)
            _signs(s, NWCH)
        se_p4 = se.n
        assert se_p4 == se4_0 + NT * SE_PER

        if "sttap" in dbg:
            @block.sync
            def _(sy):
                sy.wait_ge(ve.h, ve_p4)
                sy.wait_ge(se.h, se_p4)
                dS.dinc(sy.dma_start(out=dbg["sttap"], in_=s_st[:]))
        if "y1n" in dbg:
            @block.sync
            def _(sy):
                sy.wait_ge(dY1n[0].h, dS_p4[0])
                sy.wait_ge(dY1n[1].h, dS_p4[1])
                dS.dinc(sy.dma_start(out=dbg["y1n"], in_=d_y1s))
        if "xq" in dbg:
            @block.sync
            def _(sy):
                sy.wait_ge(dXq[0].h, dS_p4[2])
                sy.wait_ge(dXq[1].h, dS_p4[3])
                dS.dinc(sy.dma_start(out=dbg["xq"], in_=d_xqs))

        if stop_phase <= 4:
            _finish(block, ctrs)
            return nc

        ve_p5c = ve.n
        se_p5a = se.n
        dO_p5 = (dTst[0].n, dTst[1].n, dTst[2].n, dTst[3].n)
        tern_totals = [(tpA.h, tpA.n), (thrV.h, thrV.n), (thrT.h, thrT.n),
                       (ts2.h, ts2.n), (tv2.h, tv2.n),
                       (dAcc[0].h, dAcc[0].n), (dAcc[1].h, dAcc[1].n),
                       (dAcc[2].h, dAcc[2].n), (dAcc[3].h, dAcc[3].n),
                       (dW5[0].h, dW5[0].n), (dW5[1].h, dW5[1].n),
                       (dW5[2].h, dW5[2].n),
                       (dY1n[0].h, dY1n[0].n), (dY1n[1].h, dY1n[1].n),
                       (dXq[0].h, dXq[0].n), (dXq[1].h, dXq[1].n),
                       (dX[0].h, dX[0].n), (dX[1].h, dX[1].n),
                       (dAor.h, dAor.n), (dDen[0].h, dDen[0].n),
                       (dDen[1].h, dDen[1].n), (dWo.h, dWo.n),
                       (dRst[0].h, dRst[0].n), (dRst[1].h, dRst[1].n),
                       (dO.h, dO.n), (dS.h, dS.n), (dC2.h, dC2.n),
                       (dG.h, dG.n)]

        if stop_phase <= 5:
            _finish(block, ctrs)
            return nc

        te_p5 = te.n
        scD.close()
        scF.close()
        psC.close()
        scH = _scope()
        s_xql = [mk(scH, f"h_xql{i}", [128, NCH * 512], BF16) for i in range(2)]
        s_tg = [mk(scH, f"h_tg{i}", [128, NCH * 512], BF16) for i in range(2)]
        s_tu = [mk(scH, f"h_tu{i}", [128, NCH * 512], BF16) for i in range(2)]
        s_h = [mk(scH, f"h_h{i}", [128, INNER]) for i in range(NTS)]
        s_sg = [mk(scH, f"h_sg{i}", [128, 512], BF16) for i in range(2)]
        s_z2 = mk(scH, "h_z2", [128, 1024])
        s_i2 = mk(scH, "h_i2", [128, 1024], I16)
        s_u2 = mk(scH, "h_u2", [128, 1024], BF16)
        s_hq = [mk(scH, f"h_hq{i}", [128, INNER], BF16) for i in range(3)]
        s_hqT = [mk(scH, f"h_hqT{i}", [128, 32 * 128], BF16) for i in range(NTS)]
        s_td = [mk(scH, f"h_td{i}", [128, 2 * 512], BF16) for i in range(2)]
        s_y1t = [mk(scH, f"h_y1t{i}", [128, D]) for i in range(2)]
        s_os = [mk(scH, f"h_os{i}", [128, D]) for i in range(NTS)]
        s_st2 = mk(scH, "h_st2", [128, 24])
        s_sc2 = mk(scH, "h_sc2", [128, 2 * NTS * 5])

        psF = _scope()
        p_a = [psF.enter_context(nc.psum_tensor(f"p6a{i}", [128, 512], F32))
               for i in range(4)]
        p_t = [psF.enter_context(nc.psum_tensor(f"p6t{i}", [128, 512], BF16))
               for i in range(4)]

        ST2 = {k: 12 + i for i, k in enumerate(
            ["mean", "msq", "var", "rec", "rstd", "negmr", "ac", "t1", "amt"])}

        def st2(name):
            return s_st2[:, ST2[name]:ST2[name] + 1]

        def stk(name, k):
            i = {"sum": 0, "sqa": 4, "sqb": 8}[name] + k
            return s_st2[:, i:i + 1]

        SC2W = {"amx": 0, "gsc": 1, "usc": 2, "zq": 3, "dsc": 4}

        def sc2c(j, k, w):
            i = (j % 2) * NTS * 5 + k * 5 + SC2W[w]
            return s_sc2[:, i:i + 1]

        def hqb(j, k):
            return s_hq[(j * NTS + k) % 3]

        te_all6 = {}
        tn6 = te.n
        for j in range(NSUP):
            for o in range(NCH):
                for k in range(NTS):
                    tn6 += 1
                    te_all6[("gu", j, o * NTS + k, "g")] = tn6
                    tn6 += 1
                    te_all6[("gu", j, o * NTS + k, "u")] = tn6
            if j >= 1:
                for h2 in range(2):
                    for g2 in range(16):
                        tn6 += 1
                        te_all6[("dn", j - 1, h2, g2)] = tn6
            for k in range(NTS):
                for g in range(8):
                    tn6 += 1
                    te_all6[("tr", j, k, g)] = tn6
        for h2 in range(2):
            for g2 in range(16):
                tn6 += 1
                te_all6[("dn", NSUP - 1, h2, g2)] = tn6
        te6_end = tn6

        def pv(i):
            return ((i // 2) + 1) * 16

        def pv2(i):
            return ((i // 2) + 1) * 32

        dOut0 = dOut.n

        ve_ops, se_ops = [], []

        def V(key, fn, waits=(), drain=False):
            ve_ops.append((key, fn, waits, drain))

        def SEO(key, fn, waits=(), drain=False):
            se_ops.append((key, fn, waits, drain))

        def qhead(j, k):
            V(None, lambda v, j=j, k=k: v.tensor_tensor(
                st2("ac"), sc2c(j, k, "amx"), sc2c(j, k, "usc"), ALU.mult),
                drain=True)
            V(None, lambda v: v.tensor_scalar(
                st2("ac"), st2("ac"), CC(1e-5), CC(1.0), ALU.max, ALU.mult),
                drain=True)
            V(None, lambda v: v.reciprocal(st2("rec"), st2("ac")), drain=True)
            V(None, lambda v, j=j, k=k: v.tensor_tensor(
                st2("t1"), sc2c(j, k, "usc"), st2("rec"), ALU.mult),
                drain=True)
            V(None, lambda v, j=j, k=k: v.tensor_scalar(
                sc2c(j, k, "zq"), st2("t1"), CC(127.0), CC(0.0),
                ALU.mult, ALU.add), drain=True)
            V(None, lambda v, j=j, k=k: v.tensor_tensor(
                st2("msq"), st2("ac"), s_thr[:, 2:3], ALU.mult), drain=True)
            V(("qh", j, k), lambda v, j=j, k=k: v.tensor_scalar(
                sc2c(j, k, "dsc"), st2("msq"), CC(1.0 / 127.0), CC(0.0),
                ALU.mult, ALU.add), drain=True)

        def qchunks(j, k):
            gt = j * NTS + k
            for c in range(4):
                cols = slice(c * 1024, (c + 1) * 1024)
                w = []
                if c == 0 and gt >= 3:
                    pj, pk = divmod(gt - 3, NTS)
                    w.append(("te", ("tr", pj, pk, 7)))
                V(("z2", j, k, c), lambda v, j=j, k=k, c=c, cols=cols:
                    v.tensor_scalar(s_z2[:], s_h[k][:, cols],
                                    sc2c(j, k, "zq"), CC(512.0),
                                    ALU.mult, ALU.add),
                    waits=w, drain=True)
                V(None, lambda v: v.tensor_copy(s_i2[:], s_z2[:]),
                    drain=True)
                V(None, lambda v: v.scalar_tensor_tensor(
                    s_u2[:], s_z2[:], -0.5, s_i2[:], ALU.add, ALU.is_ge),
                    drain=True)
                V(("hq", j, k, c), lambda v, j=j, k=k, c=c, cols=cols:
                    v.scalar_tensor_tensor(hqb(j, k)[:, cols], s_i2[:],
                                           -512.0, s_u2[:], ALU.add,
                                           ALU.add), drain=True)

        def osbatch(jd, h2):
            for k in range(NTS):
                w = []
                if k == 0:
                    w.append(("te", ("dn", jd, h2, 15)))
                if h2 == 0 and jd >= 1:
                    w.append(("dma", dOut, dOut0 + jd * NTS * 16))
                half = slice(h2 * 512, (h2 + 1) * 512)
                V(("osh" + str(h2), jd, k), lambda v, jd=jd, k=k, half=half:
                    v.tensor_scalar(s_os[k][:, half], p_a[k][:],
                                    sc2c(jd, k, "dsc"), CC(0.0),
                                    ALU.mult, ALU.add), waits=w)

        def chainA(jd, k):
            gt = jd * NTS + k
            V(("y2", jd, k), lambda v, jd=jd, k=k: v.tensor_tensor(
                s_os[k][:], s_os[k][:], s_y1t[k % 2][:], ALU.add),
                waits=[("dma", dY1l[gt % 2], pv(gt))], drain=True)
            V(("y2s", jd, k), lambda v, k=k: v.tensor_scalar(
                s_os[k][:], s_os[k][:], CC(1.0), CC(0.0), ALU.mult, ALU.add,
                accum_out=stk("sum", k)), drain=True)

        def chainB(jd, k):
            V(None, lambda v, k=k: v.tensor_scalar(
                st2("mean"), stk("sum", k), CC(1.0 / 1024), CC(0.0),
                ALU.mult, ALU.add), drain=True)
            V(None, lambda v: v.tensor_tensor(
                st2("msq"), st2("mean"), st2("mean"), ALU.mult), drain=True)
            V(None, lambda v, k=k: v.tensor_tensor(
                st2("var"), stk("sqa", k), stk("sqb", k), ALU.add),
                waits=[("se", ("sqb", jd, k))], drain=True)
            V(None, lambda v: v.scalar_tensor_tensor(
                st2("var"), st2("var"), 1.0 / 1024, st2("msq"),
                ALU.mult, ALU.subtract), drain=True)
            V(None, lambda v: v.tensor_scalar(
                st2("var"), st2("var"), CC(1e-5), CC(1.0), ALU.add,
                ALU.mult), drain=True)
            V(("rec", jd, k), lambda v: v.reciprocal(st2("rec"), st2("var")),
                drain=True)
            V(None, lambda v: v.tensor_tensor(
                st2("negmr"), st2("mean"), st2("rstd"), ALU.mult),
                waits=[("se", ("rstd", jd, k))], drain=True)
            V(None, lambda v: v.tensor_scalar(
                st2("negmr"), st2("negmr"), CC(-1.0), CC(0.0), ALU.mult,
                ALU.add), drain=True)
            V(("out", jd, k), lambda v, k=k: v.tensor_scalar(
                s_os[k][:], s_os[k][:], st2("rstd"), st2("negmr"),
                ALU.mult, ALU.add), drain=True)

        def evacs(j, lo, hi):
            for tg_i in range(lo, hi):
                k, g = divmod(tg_i, 8)
                V(("evac", j, tg_i), lambda v, j=j, k=k, g=g, tg_i=tg_i:
                    v.tensor_copy(s_hqT[k][:, g * 512:(g + 1) * 512],
                                  p_t[tg_i % 4][:]),
                    waits=[("te", ("tr", j, k, g))])

        for j in range(NSUP):
            for k in range(NTS):
                nt = j * NTS + k
                V(None, lambda v, j=j, k=k, nt=nt: v.tensor_tensor(
                    sc2c(j, k, "gsc"), s_amx[:, nt:nt + 1], s_thr[:, 0:1],
                    ALU.mult), drain=True)
                V(None, lambda v, j=j, k=k: v.tensor_scalar(
                    sc2c(j, k, "gsc"), sc2c(j, k, "gsc"), CC(1.0 / 127.0),
                    CC(0.0), ALU.mult, ALU.add), drain=True)
                V(None, lambda v, j=j, k=k, nt=nt: v.tensor_tensor(
                    sc2c(j, k, "usc"), s_amx[:, nt:nt + 1], s_thr[:, 1:2],
                    ALU.mult), drain=True)
                V(("prep", j, k), lambda v, j=j, k=k: v.tensor_scalar(
                    sc2c(j, k, "usc"), sc2c(j, k, "usc"), CC(1.0 / 127.0),
                    CC(0.0), ALU.mult, ALU.add), drain=True)
            for gi in range(32):
                o, k = divmod(gi, NTS)
                V(("hTT", j, gi), lambda v, j=j, gi=gi, o=o, k=k: v.tensor_tensor(
                    s_h[k][:, o * 512:(o + 1) * 512], s_sg[gi % 2][:],
                    p_a[1 if gi % 2 == 0 else 3][:], ALU.mult),
                    waits=[("te", ("gu", j, gi, "u")), ("se", ("silu", j, gi))])
                sl = slice(o * 512, (o + 1) * 512)
                if o == 0:
                    V(None, lambda v, j=j, k=k, sl=sl: v.tensor_reduce(
                        sc2c(j, k, "amx"), s_h[k][:, sl], AXX, ALU.max,
                        apply_absolute_value=True), drain=True)
                else:
                    V(None, lambda v, j=j, k=k, sl=sl: v.tensor_reduce(
                        st2("amt"), s_h[k][:, sl], AXX, ALU.max,
                        apply_absolute_value=True), drain=True)
                    key = ("amax", j, k) if o == NCH - 1 else None
                    V(key, lambda v, j=j, k=k: v.tensor_tensor(
                        sc2c(j, k, "amx"), sc2c(j, k, "amx"), st2("amt"),
                        ALU.max), drain=True)
                if o == NCH - 1:
                    qhead(j, k)

            qchunks(j, 0)
            if j >= 1:
                osbatch(j - 1, 0)
            qchunks(j, 1)
            qchunks(j, 2)
            evacs(j, 0, 8)
            qchunks(j, 3)
            evacs(j, 8, 16)
            if j >= 1:
                osbatch(j - 1, 1)
            evacs(j, 16, 24)
            if j >= 1:
                chainA(j - 1, 0)
                chainA(j - 1, 1)
            evacs(j, 24, 32)
            if j >= 1:
                chainA(j - 1, 2)
                chainA(j - 1, 3)
                for k in range(NTS):
                    chainB(j - 1, k)
        osbatch(NSUP - 1, 0)
        osbatch(NSUP - 1, 1)
        for k in range(NTS):
            chainA(NSUP - 1, k)
        for k in range(NTS):
            chainB(NSUP - 1, k)

        for j in range(NSUP):
            for gi in range(32):
                o, k = divmod(gi, NTS)
                w = [("te", ("gu", j, gi, "g"))]
                if gi == 0:
                    w.append(("ve", ("prep", j, NTS - 1)))
                if gi >= 2:
                    w.append(("ve", ("hTT", j, gi - 2)))
                elif j >= 1:
                    w.append(("ve", ("hTT", j - 1, 30 + gi)))
                SEO(("silu", j, gi), lambda s, j=j, gi=gi, k=k: s.activation(
                    s_sg[gi % 2][:], p_a[0 if gi % 2 == 0 else 2][:], AF.Silu,
                    scale=sc2c(j, k, "gsc")), waits=w,
                    drain=(gi <= 1))

            def ln2sq(jd, k):
                SEO(None, lambda s, k=k: s.activation(
                    s_sg[0][:], s_os[k][:, 0:512], AF.Square,
                    accum_out=stk("sqa", k)),
                    waits=[("ve", ("y2s", jd, k))], drain=True)
                SEO(("sqb", jd, k), lambda s, k=k: s.activation(
                    s_sg[1][:], s_os[k][:, 512:1024], AF.Square,
                    accum_out=stk("sqb", k)), drain=True)

            def ln2rt(jd, k):
                SEO(("rstd", jd, k), lambda s: s.activation(
                    st2("rstd"), st2("rec"), AF.Sqrt),
                    waits=[("ve", ("rec", jd, k))], drain=True)

            if j >= 1:
                for k in range(NTS):
                    ln2sq(j - 1, k)
                for k in range(NTS):
                    ln2rt(j - 1, k)
        for k in range(NTS):
            ln2sq(NSUP - 1, k)
        for k in range(NTS):
            ln2rt(NSUP - 1, k)

        ve6_0, se6_0 = ve.n, se.n
        ve_evt = {}
        for i, (key, _, _, _) in enumerate(ve_ops):
            if key is not None:
                ve_evt[key] = ve6_0 + i + 1
        se_evt = {}
        for i, (key, _, _, _) in enumerate(se_ops):
            if key is not None:
                se_evt[key] = se6_0 + i + 1

        def _w(eng, w):
            if w[0] == "te":
                eng.wait_ge(te.h, te_all6[w[1]])
            elif w[0] == "ve":
                eng.wait_ge(ve.h, ve_evt[w[1]])
            elif w[0] == "se":
                eng.wait_ge(se.h, se_evt[w[1]])
            elif w[0] == "dma":
                eng.wait_ge(w[1].h, w[2])

        def _xql_load(sy, j):
            inap = bass.AP(d_xqs.tensor, j * 512,
                           [[S * 128, NCH], [S, 128], [1, 512]])
            outap = bass.AP(s_xql[j % 2][:].tensor, s_xql[j % 2][:].offset,
                            [[NCH * 512, 128], [512, NCH], [1, 512]])
            dXql[j % 2].dinc(sy.dma_start(out=outap, in_=inap))
            assert dXql[j % 2].n == pv(j)

        def _slice_load(sy, j, o):
            gs = j * NCH + o
            for dsrc2, buf in ((d_terng, s_tg), (d_ternu, s_tu)):
                inap = bass.AP(dsrc2.tensor, o * 512,
                               [[INNER * 128, NCH], [INNER, 128], [1, 512]])
                outap = bass.AP(buf[gs % 2][:].tensor, buf[gs % 2][:].offset,
                                [[NCH * 512, 128], [512, NCH], [1, 512]])
                dGu[gs % 2].dinc(sy.dma_start(out=outap, in_=inap))
            assert dGu[gs % 2].n == pv2(gs)

        @block.sync
        def _(sy):
            sy.wait_ge(ve.h, ve_p5c)
            sy.wait_ge(se.h, se_p5a)
            sy.wait_ge(te.h, te_p5)
            for _i6 in range(4):
                sy.wait_ge(dTst[_i6].h, dO_p5[_i6])
            for _h6, _v6 in tern_totals:
                if _v6:
                    sy.wait_ge(_h6, _v6)
            _xql_load(sy, 0)
            _xql_load(sy, 1)
            _slice_load(sy, 0, 0)
            _slice_load(sy, 0, 1)
            for j in range(NSUP):
                for o in range(2, NCH):
                    sy.wait_ge(te.h, te_all6[("gu", j, (o - 2) * NTS + NTS - 1, "u")])
                    _slice_load(sy, j, o)
                if j + 2 < NSUP:
                    sy.wait_ge(te.h, te_all6[("gu", j, 31, "u")])
                    _xql_load(sy, j + 2)
                if j + 1 < NSUP:
                    sy.wait_ge(te.h, te_all6[("gu", j, 27, "u")])
                    _slice_load(sy, j + 1, 0)
                    sy.wait_ge(te.h, te_all6[("gu", j, 31, "u")])
                    _slice_load(sy, j + 1, 1)
                if j >= 1:
                    for k in range(NTS):
                        _w(sy, ("ve", ("out", j - 1, k)))
                        nt = (j - 1) * NTS + k
                        dOut.dinc(sy.dma_start(
                            out=d_out[nt * 128:(nt + 1) * 128, :],
                            in_=s_os[k][:]))
            for k in range(NTS):
                _w(sy, ("ve", ("out", NSUP - 1, k)))
                nt = (NSUP - 1) * NTS + k
                dOut.dinc(sy.dma_start(
                    out=d_out[nt * 128:(nt + 1) * 128, :], in_=s_os[k][:]))

        @block.gpsimd
        def _(g):
            g.wait_ge(ve.h, ve_p5c)
            g.wait_ge(se.h, se_p5a)
            g.wait_ge(te.h, te_p5)
            for _i6 in range(4):
                g.wait_ge(dTst[_i6].h, dO_p5[_i6])
            for _h6, _v6 in tern_totals:
                if _v6:
                    g.wait_ge(_h6, _v6)
            g.wait_ge(dY1n[0].h, dS_p4[0])
            g.wait_ge(dY1n[1].h, dS_p4[1])

            def _td(jd, h2, g2):
                dn = jd * 32 + h2 * 16 + g2
                if dn >= 2:
                    pdn = dn - 2
                    g.wait_ge(te.h, te_all6[("dn", pdn // 32,
                                             (pdn % 32) // 16, pdn % 16)])
                inap = bass.AP(d_ternd.tensor, g2 * 256 * D + h2 * 512,
                               [[D, 128], [128 * D, 2], [1, 512]])
                buf = s_td[dn % 2]
                outap = bass.AP(buf[:].tensor, buf[:].offset,
                                [[2 * 512, 128], [512, 2], [1, 512]])
                dTd[dn % 2].dinc(g.dma_start(out=outap, in_=inap))
                assert dTd[dn % 2].n == pv(dn)

            def _y1(jd, k):
                gt = jd * NTS + k
                if gt >= 2:
                    g.wait_ge(ve.h, ve_evt[("y2", (gt - 2) // NTS,
                                            (gt - 2) % NTS)])
                dY1l[gt % 2].dinc(g.dma_start(
                    out=s_y1t[k % 2][:], in_=d_y1s[gt * 128:(gt + 1) * 128, :]))
                assert dY1l[gt % 2].n == pv(gt)

            for jd in range(NSUP):
                for h2 in range(2):
                    for g2 in range(16):
                        _td(jd, h2, g2)
                for k in range(NTS):
                    _y1(jd, k)

        @block.tensor
        def _(t):
            def emit_down(jd, final):
                for h2 in range(2):
                    for g2 in range(16):
                        dn = jd * 32 + h2 * 16 + g2
                        t.wait_ge(dTd[dn % 2].h, pv(dn))
                        if h2 == 0 and g2 == 0:
                            t.wait_ge(ve.h, ve_evt[("evac", jd, 31)])
                        for kk in range(2):
                            for k in range(NTS):
                                if kk == 0 and g2 == 0:
                                    if h2 == 1:
                                        t.wait_ge(ve.h, ve_evt[("osh0", jd, k)])
                                    elif final:
                                        t.wait_ge(ve.h, ve_evt[("osh1", jd - 1, k)])
                                    else:
                                        jn = jd + 1
                                        if k == 0:
                                            t.wait_ge(se.h, se_evt[("silu", jn, 30)])
                                        elif k == 1:
                                            t.wait_ge(ve.h, ve_evt[("hTT", jn, 30)])
                                        elif k == 2:
                                            t.wait_ge(se.h, se_evt[("silu", jn, 31)])
                                        else:
                                            t.wait_ge(ve.h, ve_evt[("hTT", jn, 31)])
                                kc = g2 * 2 + kk
                                mm = t.matmul(
                                    p_a[k][:],
                                    s_hqT[k][:, kc * 128:(kc + 1) * 128],
                                    s_td[dn % 2][:, kk * 512:(kk + 1) * 512],
                                    start=(g2 == 0 and kk == 0),
                                    stop=(g2 == 15 and kk == 1))
                        te.inc(mm)
                        assert te.n == te_all6[("dn", jd, h2, g2)]

            for j in range(NSUP):
                for o in range(NCH):
                    gs = j * NCH + o
                    t.wait_ge(dGu[gs % 2].h, pv2(gs))
                    if o == 0:
                        t.wait_ge(dXql[j % 2].h, pv(j))
                    for k in range(NTS):
                        gi = o * NTS + k
                        pg = p_a[0 if gi % 2 == 0 else 2]
                        pu = p_a[1 if gi % 2 == 0 else 3]
                        if gi >= 2:
                            t.wait_ge(se.h, se_evt[("silu", j, gi - 2)])
                        elif j >= 2:
                            t.wait_ge(ve.h, ve_evt[("osh1", j - 2, 0 if gi == 0 else 2)])
                        elif j == 1:
                            t.wait_ge(se.h, se_evt[("silu", 0, 30 + gi)])
                        for kc in range(NCH):
                            mm = t.matmul(
                                pg[:],
                                s_xql[j % 2][:, kc * 512 + k * 128:kc * 512 + (k + 1) * 128],
                                s_tg[gs % 2][:, kc * 512:(kc + 1) * 512],
                                start=(kc == 0), stop=(kc == NCH - 1))
                        te.inc(mm)
                        assert te.n == te_all6[("gu", j, gi, "g")]
                        if gi >= 2:
                            t.wait_ge(ve.h, ve_evt[("hTT", j, gi - 2)])
                        elif j >= 2:
                            t.wait_ge(ve.h, ve_evt[("osh1", j - 2, 1 if gi == 0 else 3)])
                        elif j == 1:
                            t.wait_ge(ve.h, ve_evt[("hTT", 0, 30 + gi)])
                        for kc in range(NCH):
                            mm = t.matmul(
                                pu[:],
                                s_xql[j % 2][:, kc * 512 + k * 128:kc * 512 + (k + 1) * 128],
                                s_tu[gs % 2][:, kc * 512:(kc + 1) * 512],
                                start=(kc == 0), stop=(kc == NCH - 1))
                        te.inc(mm)
                        assert te.n == te_all6[("gu", j, gi, "u")]
                if j >= 1:
                    emit_down(j - 1, final=False)
                for k in range(NTS):
                    for g in range(8):
                        tg_i = k * 8 + g
                        if g == 0:
                            t.wait_ge(ve.h, ve_evt[("hq", j, k, 3)])
                        if tg_i >= 4:
                            t.wait_ge(ve.h, ve_evt[("evac", j, tg_i - 4)])
                        elif j >= 1:
                            t.wait_ge(ve.h, ve_evt[("evac", j - 1, 28 + tg_i)])
                        pb = p_t[tg_i % 4]
                        for kk in range(4):
                            kc = g * 4 + kk
                            mm = t.transpose(pb[:, kk * 128:(kk + 1) * 128],
                                             hqb(j, k)[:, kc * 128:(kc + 1) * 128],
                                             s_ident[:])
                        te.inc(mm)
                        assert te.n == te_all6[("tr", j, k, g)]
            emit_down(NSUP - 1, final=True)
            assert te.n == te6_end

        @block.vector
        def _(v):
            for key, fn, waits, drain in ve_ops:
                for w in waits:
                    _w(v, w)
                if drain:
                    v.drain()
                ve.inc(fn(v))
        assert ve.n == ve6_0 + len(ve_ops)

        @block.scalar
        def _(s):
            for key, fn, waits, drain in se_ops:
                for w in waits:
                    _w(s, w)
                if drain:
                    s.drain()
                se.inc(fn(s))
        assert se.n == se6_0 + len(se_ops)

        _finish(block, ctrs)
        return nc

def _finish(block, ctrs):
    @block.sync
    def _(sy):
        for c in ctrs:
            if c.n:
                sy.wait_ge(c.h, c.n)



def _host_constants():
    cons = np.tile(np.array(CONST_VALS, dtype=np.float32), (128, 1))
    ident = np.eye(128, dtype=ml_dtypes.bfloat16)
    inv_freq = (1.0 / (10000.0 ** (np.arange(0, DH, 2, dtype=np.float32) / DH))).astype(np.float32)
    t = np.arange(S, dtype=np.float32)
    freqs = t[None, :] * inv_freq[:, None]
    cos32 = np.cos(freqs).astype(np.float32)
    sin32 = np.sin(freqs).astype(np.float32)
    cosf = np.tile(cos32, (4, 1))
    sinf = np.concatenate([-sin32, sin32, -sin32, sin32], axis=0)
    kl = np.arange(128)[:, None]
    ql = np.arange(512)[None, :]
    masks = np.concatenate(
        [np.where(128 * r + kl > ql, np.float32(-1e38), np.float32(0.0))
         for r in range(4)], axis=1).astype(ml_dtypes.bfloat16)
    sel2 = np.zeros((2, 128), dtype=np.float32)
    sel2[0, 0:64] = 1.0
    sel2[1, 64:128] = 1.0
    ones1 = np.ones((1, 128), dtype=np.float32)
    return cons, ident, np.ascontiguousarray(cosf), np.ascontiguousarray(sinf), masks, sel2, ones1


def kernel(L0_attn_x, L0_attn_w_qkv, L0_attn_w_out, L0_ffn_w_gate, L0_ffn_w_up,
           L0_ffn_w_down):
    x = np.ascontiguousarray(np.asarray(L0_attn_x, dtype=np.float32))
    wqkv = np.ascontiguousarray(np.asarray(L0_attn_w_qkv, dtype=np.float32))
    wout = np.ascontiguousarray(np.asarray(L0_attn_w_out, dtype=np.float32))
    wgT = np.ascontiguousarray(np.asarray(L0_ffn_w_gate, dtype=np.float32).T)
    wuT = np.ascontiguousarray(np.asarray(L0_ffn_w_up, dtype=np.float32).T)
    wdT = np.ascontiguousarray(np.asarray(L0_ffn_w_down, dtype=np.float32).T)
    cons, ident, cosf, sinf, masks, sel2, ones1 = _host_constants()

    nc = build_program()
    in_maps = []
    for b in range(8):
        xb = np.ascontiguousarray(x[b])
        in_maps.append({
            "xT": np.ascontiguousarray(xb.T), "x": xb,
            "wqkv": wqkv, "wout": wout, "wgT": wgT, "wuT": wuT, "wdT": wdT,
            "consts": cons, "cosf": cosf, "sinf": sinf, "ident": ident,
            "masks": masks, "sel2": sel2, "ones1": ones1,
        })
    res = run_bass_kernel_spmd(nc, in_maps, core_ids=list(range(8)))
    return np.stack([res.results[b]["out"] for b in range(8)], axis=0)
